# version 13
# baseline (speedup 1.0000x reference)
import numpy as np
import concourse.bass as bass
import concourse.mybir as mybir
from concourse.bass_utils import run_bass_kernel_spmd
from contextlib import ExitStack

F32 = mybir.dt.float32
BF16 = mybir.dt.bfloat16
I32 = mybir.dt.int32
AF = mybir.ActivationFunctionType
ALU = mybir.AluOpType
AX = mybir.AxisListType

NCORES = 8
T = 2048
TC = 16
D = 1024
KC = 8
S = 16384
PI = float(np.pi)
INV2PI = float(1.0 / (2 * np.pi))
TWO_PI = float(2 * np.pi)
RS128 = float(1.0 / np.sqrt(128.0))
MAGIC = 12582912.0
CEXP = float(np.sqrt(128.0))

f32 = np.float32


class Sem:
    def __init__(self, es, nc, name):
        self.h = es.enter_context(nc.semaphore(name))
        self.n = 0

    def inc(self, instr, k=1):
        instr.then_inc(self.h, k)
        self.n += k
        return self.n

    def wait(self, eng, n=None):
        eng.wait_ge(self.h, self.n if n is None else n)


class Mega:
    def __init__(self, upto=99, debug=()):
        self.upto = upto
        self.debug = set(debug)
        self.dbg_pending = []
        nc = bass.Bass("TRN2", target_bir_lowering=False)
        self.nc = nc
        self.es = ExitStack()
        self.build()

    def din(self, name, shape, dtype=F32):
        return self.nc.dram_tensor(name, list(shape), dtype, kind="ExternalInput").ap()

    def dout(self, name, shape, dtype=F32):
        return self.nc.dram_tensor(name, list(shape), dtype, kind="ExternalOutput").ap()

    def dram(self, name, shape, dtype=F32, shared=False):
        return self.nc.dram_tensor(name, list(shape), dtype, kind="Internal",
                                   addr_space=("Shared" if shared else "Local")).ap()

    def sb(self, name, shape, dtype=F32):
        return self.es.enter_context(self.nc.sbuf_tensor(name, list(shape), dtype))

    def dbg(self, name, src_ap, sem):
        if name not in self.debug:
            return
        val = sem.n

        def emit(sync):
            out = self.dout("dbg_" + name, list(src_ap.shape), src_ap.dtype)
            sync.wait_ge(sem.h, val)
            self.sst.inc(sync.dma_start(out=out[:], in_=src_ap), 16)
        self.blk.sync(emit)

    def flush_dbg(self):
        pass

    def build(self):
        nc, es = self.nc, self.es
        U = self.upto

        x_in = self.din("x_slice", [T, D])
        self.wi = dict(
            wq0=self.din("L0_Wq", [D, D]), wk0=self.din("L0_Wk", [D, D]),
            wv0T=self.din("L0_WvT", [D, D]), wo0=self.din("L0_Wo", [D, D]),
            wfslab=self.din("F0_w_slab", [128, D]),
            wq1T=self.din("L1_WqT", [D, 128]), wk1T=self.din("L1_WkT", [D, 128]),
            wv1T=self.din("L1_WvT", [D, 128]), wo1T=self.din("L1_WoT", [128, D]),
            wr1T=self.din("F1_WrT", [D, D]), wkf1T=self.din("F1_WkT", [D, D]),
            wvf1T=self.din("F1_WvT", [D, D]),
        )
        bo0 = self.din("L0_bo_row", [1, D])
        bf0 = self.din("F0_bias_row", [1, D])
        bq1 = self.din("L1_bq_pp", [128, 1]); bk1 = self.din("L1_bk_pp", [128, 1])
        bv1r = self.din("L1_bv_row", [1, 128]); bo1r = self.din("L1_bo_row", [1, D])
        wav = self.din("L1_wav_pp", [128, 1]); phb = self.din("L1_phb_pp", [128, 1])
        mur = self.din("F1_mur_pp", [128, KC]); muk = self.din("F1_muk_pp", [128, KC])
        identb = self.din("identb", [128, 128], BF16)
        mask_dd = self.din("mask_dd", [128, 128], BF16)
        e16sel = self.din("e16sel", [128, 16], BF16)
        mask128 = self.din("mask128", [128, 128], BF16)
        s_all = self.din("s_all", [16, KC * 128], BF16)
        hs = self.din("hs", [16, 2], BF16)
        ones_col = self.din("ones_col", [1, 128])
        ones_col_b = self.din("ones_col_b", [1, 128], BF16)
        halo_sel = self.din("halo_sel", [8, 1], BF16)

        self.y_out = self.dout("y", [T, D])

        self.d_ln0_in = self.dram("b_ln0_in", [KC * 128, T], BF16)
        self.d_ln0_out = self.dram("b_ln0_out", [NCORES * KC * 128, T], BF16, shared=True)
        self.d_en_in = self.dram("b_en_in", [16, 16])
        self.d_en_out = self.dram("b_en_out", [16, 16])
        self.d_wf_in = self.dram("b_wf_in", [KC, 128 * 128], BF16)
        self.d_wf_out = self.dram("b_wf_out", [NCORES * KC, 128 * 128], BF16, shared=True)
        self.d_kv_in = self.dram("b_kv_in", [3 * 128, T], BF16)
        self.d_kv_out = self.dram("b_kv_out", [NCORES * 3 * 128, T], BF16, shared=True)
        self.d_halo_in = self.dram("b_halo_in", [1, D], BF16)
        self.d_halo_out = self.dram("b_halo_out", [NCORES, D], BF16, shared=True)

        self.XM = self.sb("XM", [128, TC * 1024], F32)
        self.FMA = self.sb("FMA", [128, 16384], BF16)
        self.UTM = self.sb("UTM", [128, 16384], BF16)
        self.SCRF = self.sb("SCRF", [128, 3072], F32)
        self.SCRB = self.sb("SCRB", [128, 8192], BF16)
        self.WS1 = self.sb("WS1", [128, 8192], BF16)
        self.WS2 = self.sb("WS2", [128, 8192], BF16)
        self.SM = self.sb("SM", [128, 384], F32)
        self.SCRF2 = self.sb("SCRF2", [128, 1024], F32)
        CB = self.sb("CB", [128, 400], BF16)
        self.SALL = self.sb("SALL", [16, KC * 128], BF16)
        self.HS = self.sb("HS", [16, 2], BF16)
        self.HSEL = self.sb("HSEL", [8, 1], BF16)
        self.PP = self.sb("PP", [128, 32], F32)
        self.ONEC = self.sb("ONEC", [1, 128], F32)
        self.ONEB = self.sb("ONEBt", [1, 128], BF16)
        self.BROWS = self.sb("BROWS", [1, 5376], BF16)
        self.MASK_DD = CB[:, 0:128]
        self.MASK128 = CB[:, 128:256]
        self.E16 = CB[:, 256:272]
        self.IDB = CB[:, 272:400]

        self.sq = Sem(es, nc, "s_q")
        self.sg = Sem(es, nc, "s_g")
        self.smm = Sem(es, nc, "s_mm")
        self.sdv = Sem(es, nc, "s_dv")
        self.sac = Sem(es, nc, "s_ac")
        self.scc = Sem(es, nc, "s_cc")
        self.scc1 = Sem(es, nc, "s_cc1")
        self.sqP = [Sem(es, nc, "s_qp0"), Sem(es, nc, "s_qp1")]
        self.scc2 = Sem(es, nc, "s_cc2")
        self.scc3 = Sem(es, nc, "s_cc3")
        self.scc4 = Sem(es, nc, "s_cc4")
        self.sst = Sem(es, nc, "s_st")
        sq = self.sq

        blk = es.enter_context(nc.Block())
        self.blk = blk

        self.sxq = [Sem(es, nc, f"s_x{i}") for i in range(4)]
        xm_t = self.XM.ap().rearrange("p (t d) -> p t d", d=D)
        xin_t = x_in.rearrange("(t p) d -> p t d", p=128)

        def init_loads(sync):
            q = lambda out, in_: sq.inc(sync.dma_start(out=out, in_=in_), 16)
            self.sxq[0].inc(sync.dma_start(
                out=xm_t[:, 0:4], in_=xin_t[:, 0:4]), 16)
            self.sxq[3].inc(sync.dma_start(
                out=xm_t[:, 12:16], in_=xin_t[:, 12:16]), 16)
            q(CB[:, 0:128], mask_dd[:])
            q(CB[:, 128:256], mask128[:])
            q(CB[:, 256:272], e16sel[:])
            q(CB[:, 272:400], identb[:])
            q(self.SALL[:], s_all[:])
            q(self.HS[:], hs[:])
            q(self.HSEL[:], halo_sel[:])
            q(self.ONEC[:], ones_col[:])
            q(self.ONEB[:], ones_col_b[:])
            q(self.PP[:, 0:KC], mur[:])
            q(self.PP[:, 8:16], muk[:])
            q(self.PP[:, 16:17], wav[:])
            q(self.PP[:, 17:18], phb[:])
            q(self.PP[:, 18:19], bq1[:])
            q(self.PP[:, 19:20], bk1[:])
        blk.sync(init_loads)
        self.loads_done = sq.n

        def xload_act(scalar):
            self.sxq[1].inc(scalar.dma_start(
                out=xm_t[:, 4:8], in_=xin_t[:, 4:8]), 16)
            self.sxq[2].inc(scalar.dma_start(
                out=xm_t[:, 8:12], in_=xin_t[:, 8:12]), 16)
        blk.scalar(xload_act)

        def brow_loads(gpsimd):
            g = lambda out, in_: self.sg.inc(gpsimd.dma_start(out=out, in_=in_), 16)
            g(self.BROWS[0:1, 0:D], bo0[:])
            g(self.BROWS[0:1, D:2 * D], bf0[:])
            g(self.BROWS[0:1, 2 * D:3 * D], bo1r[:])
            g(self.BROWS[0:1, 3 * D:3 * D + 128], bv1r[:])
        blk.gpsimd(brow_loads)
        self.brows_done_g = self.sg.n

        self.phase_ln0()
        if U >= 2:
            self.phase_l0_attn()
        if U >= 3:
            self.phase_l0_ffn()
        if U >= 4:
            self.phase_l1_attn()
        if U >= 5:
            self.phase_l1_ffn()

        self.flush_dbg()

        def fin(sync):
            self.sdv.wait(sync)
            self.sac.wait(sync)
            self.smm.wait(sync)
            self.sst.inc(sync.dma_start(
                out=self.y_out.rearrange("(t p) d -> p t d", p=128),
                in_=self.XM.ap().rearrange("p (t d) -> p t d", d=D)), 16)
            self.sst.wait(sync)
        blk.sync(fin)

    def ln_stats(self, stats, extra_minmax=False, wait=None, chunk_wait=None):
        blk = self.blk
        XM = self.XM
        bn = self.SCRF[:, 0:192]
        CM, CV, CI, CN, CX, CNM, CSC = 0, 16, 32, 48, 64, 80, 96
        sdv, sac = self.sdv, self.sac

        def stats_dve(vector):
            if wait is not None:
                wait(vector)
            for t in range(TC):
                if chunk_wait is not None:
                    chunk_wait(vector, t)
                xs = XM[:, 1024 * t:1024 * (t + 1)]
                vector.bn_stats(bn[:, 12 * t:12 * t + 6], xs[:, 0:512])
                ins = vector.bn_stats(bn[:, 12 * t + 6:12 * t + 12], xs[:, 512:1024])
                if extra_minmax:
                    vector.tensor_reduce(stats[:, CX + t:CX + t + 1], xs, AX.X, ALU.max)
                    ins = vector.tensor_reduce(stats[:, CNM + t:CNM + t + 1], xs,
                                               AX.X, ALU.min, negate=True)
            sdv.inc(ins)
            sdv.wait(vector)
            for t in range(TC):
                ins = vector.bn_aggr(stats[:, CSC + 2 * (t % 8):CSC + 2 * (t % 8) + 2],
                                     bn[:, 12 * t:12 * t + 12])
                sdv.inc(ins)
                sdv.wait(vector)
                ins = vector.tensor_copy(stats[:, CM + t:CM + t + 1],
                                         stats[:, CSC + 2 * (t % 8):CSC + 2 * (t % 8) + 1])
                ins = vector.tensor_scalar(
                    stats[:, CV + t:CV + t + 1],
                    stats[:, CSC + 2 * (t % 8) + 1:CSC + 2 * (t % 8) + 2],
                    1.0, 1e-5, ALU.mult, ALU.add)
            sdv.inc(ins)
        blk.vector(stats_dve)

        def stats_sqrt(scalar):
            sdv.wait(scalar)
            self.sac.inc(scalar.activation(
                stats[:, CI:CI + 16], stats[:, CV:CV + 16], AF.Sqrt))
        blk.scalar(stats_sqrt)

        def stats_fin(vector):
            sac.wait(vector)
            std = stats[:, CI:CI + 16]
            inv0 = stats[:, CSC:CSC + 16]
            sdv.inc(vector.reciprocal(inv0, std))
            sdv.wait(vector)
            a = stats[:, CSC + 16:CSC + 32]
            sdv.inc(vector.tensor_mul(a, inv0, inv0))
            sdv.wait(vector)
            sdv.inc(vector.tensor_mul(a, a, stats[:, CV:CV + 16]))
            sdv.wait(vector)
            sdv.inc(vector.tensor_scalar(a, a, -0.5, 1.5, ALU.mult, ALU.add))
            sdv.wait(vector)
            sdv.inc(vector.tensor_mul(stats[:, CI:CI + 16], inv0, a))
            sdv.wait(vector)
            sdv.inc(vector.scalar_tensor_tensor(
                stats[:, CN:CN + 16], stats[:, CM:CM + 16], -1.0,
                stats[:, CI:CI + 16], ALU.mult, ALU.mult))
        blk.vector(stats_fin)
        return dict(inv=CI, negmi=CN, m=CM, xmax=CX, negxmin=CNM, v=CV)

    def ln_apply(self, stats, cols, dst_tm, n0=0, n1=TC):
        blk = self.blk
        base = self.sac.n

        def ap(scalar):
            self.sdv.wait(scalar)
            for t in range(n0, n1):
                self.sac.inc(scalar.activation(
                    dst_tm[:, 1024 * (t - n0):1024 * (t - n0 + 1)],
                    self.XM[:, 1024 * t:1024 * (t + 1)],
                    AF.Identity,
                    bias=stats[:, cols["negmi"] + t:cols["negmi"] + t + 1],
                    scale=stats[:, cols["inv"] + t:cols["inv"] + t + 1]))
        blk.scalar(ap)
        return base

    def transpose_tm_fm(self, tag, src_tm, fm, n_tc=TC, wait_sem=None,
                        wait_val=None, src_cols=1024, order="k",
                        group_wait=None, t_off=0, fm_kstride=None,
                        pre_wait=None):
        nc, blk = self.nc, self.blk
        es = ExitStack()
        pb = [es.enter_context(nc.psum_tensor(f"{tag}_tp{i}", [128, 512], BF16))
              for i in range(2)]
        smm, sdv = self.smm, self.sdv
        kc = src_cols // 128
        if order == "t":
            groups = [(k, t0) for t0 in range(0, n_tc, 4) for k in range(kc)]
        else:
            groups = [(k, t0) for k in range(kc) for t0 in range(0, n_tc, 4)]
        mm_base = smm.n
        dv_base = sdv.n

        def tp_mm(tensor):
            if pre_wait is not None:
                pre_wait(tensor)
            if wait_sem is not None and group_wait is None:
                tensor.wait_ge(wait_sem.h, wait_val)
            last_gw = -1
            for gi, (k, t0) in enumerate(groups):
                if group_wait is not None:
                    gv = group_wait(t0)
                    if gv > last_gw:
                        tensor.wait_ge(wait_sem.h, gv)
                        last_gw = gv
                b = pb[gi % 2]
                if gi >= 2:
                    tensor.wait_ge(sdv.h, dv_base + gi - 1)
                for j in range(4):
                    t = t0 + j
                    ins = tensor.transpose(
                        b[:, 128 * j:128 * (j + 1)],
                        src_tm[:, src_cols * t + 128 * k:src_cols * t + 128 * (k + 1)],
                        self.IDB[:])
                smm.inc(ins)
        blk.tensor(tp_mm)

        ks = fm_kstride if fm_kstride is not None else n_tc * 128

        def tp_dv(vector):
            for gi, (k, t0) in enumerate(groups):
                vector.wait_ge(smm.h, mm_base + gi + 1)
                dst = fm[:, k * ks + 128 * (t_off + t0):
                         k * ks + 128 * (t_off + t0 + 4)]
                self.sdv.inc(vector.tensor_copy(dst, pb[gi % 2][:]))
        blk.vector(tp_dv)
        es.close()
        return dv_base

    def lw_inline(self, gpsimd, hbm_ap, dst, rows, cols):
        kc = rows // 128
        self.sg.inc(gpsimd.dma_start(
            out=dst.rearrange("p (k c) -> p k c", c=cols)[:, 0:kc, :],
            in_=hbm_ap.rearrange("(k p) c -> p k c", p=128)), 16)

    def load_weight_bf16(self, hbm_ap, dst, rows, cols):
        kc = rows // 128

        def ld(gpsimd):
            self.sg.inc(gpsimd.dma_start(
                out=dst.rearrange("p (k c) -> p k c", c=cols)[:, 0:kc, :],
                in_=hbm_ap.rearrange("(k p) c -> p k c", p=128)), 16)
        self.blk.gpsimd(ld)
        return self.sg.n

    def phase_ln0(self):
        stats = self.SM[:, 0:128]

        def cw(vector, t):
            if t % 4 == 0:
                vector.wait_ge(self.sxq[t // 4].h, 16)
        self.ln0_stats_cols = self.ln_stats(stats, chunk_wait=cw)
        self.ln0_ap_base = self.ln_apply(stats, self.ln0_stats_cols, self.UTM)
        self.dbg("ln0_fm", self.FMA[:], self.sdv)
        self.dbg("ln0_stats", self.SM[:, 0:128], self.sdv)
        self.dbg("ln0_aux", self.SM[:, 128:384], self.sdv)
        self.dbg("ln0_utm", self.UTM[:], self.sac)

    def phase_l0_attn(self):
        nc, blk = self.nc, self.blk
        sq, sg, smm, sdv, sac, scc = (self.sq, self.sg, self.smm, self.sdv,
                                      self.sac, self.scc)
        FMA, UTM, SCRB, SM = self.FMA, self.UTM, self.SCRB, self.SM
        WS1, WS2 = self.WS1, self.WS2
        BROWS, ONEB = self.BROWS, self.ONEB


        wq_ld = self.load_weight_bf16(self.wi["wq0"], WS1.ap(), D, D)
        wk_ld = self.load_weight_bf16(self.wi["wk0"], WS2.ap(), D, D)


        es_en = ExitStack()
        PE1 = es_en.enter_context(nc.psum_tensor("pe1", [128, 512], F32))
        PE2 = es_en.enter_context(nc.psum_tensor("pe2", [128, 512], F32))
        es_qk = ExitStack()
        pq = es_qk.enter_context(nc.psum_tensor("pq", [128, 1024], F32))
        pk = es_qk.enter_context(nc.psum_tensor("pk", [128, 1024], F32))

        def hilo(par, which):
            base = 4096 * par
            off = {"qhi": 0, "qlo": 1024, "khi": 2048, "klo": 3072}[which]
            return SCRB[:, base + off:base + off + 1024]

        q_mm = [0] * TC; k_mm = [0] * TC
        qhi_d = [0] * TC; khi_d = [0] * TC
        qlo_d = [0] * TC; klo_d = [0] * TC
        en_d = [0] * TC

        tpb = self.transpose_tm_fm(
            "ln0", UTM, FMA, n_tc=TC, wait_sem=sac, order="t",
            group_wait=lambda t0: self.ln0_ap_base + t0 + 4,
            pre_wait=lambda tensor: tensor.wait_ge(sq.h, self.loads_done))
        tp_done = [tpb + 8 * (g + 1) for g in range(4)]

        d_in_kt = self.d_ln0_in.rearrange("(k p) t -> p k t", p=128)
        fma_kt = self.FMA.ap().rearrange("p (k t) -> p k t", t=T)

        def bw(sync):
            sync.wait_ge(sq.h, sq.n)
            for g in range(4):
                sync.wait_ge(sdv.h, tp_done[g])
                sq.inc(sync.dma_start(
                    out=d_in_kt[:, :, 512 * g:512 * (g + 1)],
                    in_=fma_kt[:, :, 512 * g:512 * (g + 1)]), 16)
        blk.sync(bw)
        ln0_bounce_done = sq.n

        def ag_ln0(gpsimd):
            gpsimd.wait_ge(sq.h, ln0_bounce_done)
            self.scc1.inc(gpsimd.collective_compute(
                "AllGather", ALU.bypass, replica_groups=[list(range(NCORES))],
                ins=[self.d_ln0_in[:]], outs=[self.d_ln0_out[:]]))
        blk.gpsimd(ag_ln0)
        ag_ln0_done = self.scc1.n

        for t in range(TC):
            par = t % 2

            def pe_qk(tensor, t=t, par=par):
                if t == 0:
                    tensor.wait_ge(sg.h, wk_ld)
                if t % 4 == 0:
                    tensor.wait_ge(sdv.h, tp_done[t // 4])
                if t >= 1:
                    tensor.wait_ge(sdv.h, qlo_d[t - 1])
                for k in range(KC):
                    for n in range(2):
                        ins = tensor.matmul(
                            pq[:, 512 * n:512 * (n + 1)],
                            FMA[:, k * T + 128 * t:k * T + 128 * (t + 1)],
                            WS1[:, 1024 * k + 512 * n:1024 * k + 512 * (n + 1)],
                            start=(k == 0), stop=(k == KC - 1))
                q_mm[t] = smm.inc(ins)
                if t >= 1:
                    tensor.wait_ge(sdv.h, klo_d[t - 1])
                for k in range(KC):
                    for n in range(2):
                        ins = tensor.matmul(
                            pk[:, 512 * n:512 * (n + 1)],
                            FMA[:, k * T + 128 * t:k * T + 128 * (t + 1)],
                            WS2[:, 1024 * k + 512 * n:1024 * k + 512 * (n + 1)],
                            start=(k == 0), stop=(k == KC - 1))
                k_mm[t] = smm.inc(ins)
            blk.tensor(pe_qk)

            def dve_evac(vector, t=t, par=par):
                if t >= 2:
                    vector.wait_ge(smm.h, en_d[t - 2])
                vector.wait_ge(smm.h, q_mm[t])
                qhi_d[t] = sdv.inc(vector.tensor_copy(hilo(par, "qhi"), pq[:]))
                sdv.wait(vector)
                qlo_d[t] = sdv.inc(vector.scalar_tensor_tensor(
                    hilo(par, "qlo"), pq[:], 1.0, hilo(par, "qhi"),
                    ALU.mult, ALU.subtract))
                vector.wait_ge(smm.h, k_mm[t])
                khi_d[t] = sdv.inc(vector.tensor_copy(hilo(par, "khi"), pk[:]))
                sdv.wait(vector)
                klo_d[t] = sdv.inc(vector.scalar_tensor_tensor(
                    hilo(par, "klo"), pk[:], 1.0, hilo(par, "khi"),
                    ALU.mult, ALU.subtract))
            blk.vector(dve_evac)

            def pe_energy(tensor, t=t, par=par):
                tensor.wait_ge(sdv.h, qlo_d[t])
                tensor.wait_ge(sdv.h, klo_d[t])
                for dg in range(8):
                    slot = (PE1 if dg < 4 else PE2)[:, 128 * (dg % 4):128 * (dg % 4) + 128]
                    for pi, (lw, rw) in enumerate(
                            [("qhi", "khi"), ("qhi", "klo"), ("qlo", "khi")]):
                        ins = tensor.matmul(
                            slot, hilo(par, lw)[:, 128 * dg:128 * (dg + 1)],
                            hilo(par, rw)[:, 128 * dg:128 * (dg + 1)],
                            start=(t == 0 and pi == 0 and dg % 4 == 0),
                            stop=(t == TC - 1 and pi == 2 and dg % 4 == 3))
                en_d[t] = smm.inc(ins)
            blk.tensor(pe_energy)

        masked = lambda dg: SCRB[:, 128 * dg:128 * (dg + 1)]

        def dve_mask(vector):
            vector.wait_ge(smm.h, en_d[TC - 1])
            for dg in range(8):
                slot = (PE1 if dg < 4 else PE2)[:, 128 * (dg % 4):128 * (dg % 4) + 128]
                ins = vector.tensor_tensor(masked(dg), slot, self.MASK_DD, ALU.mult)
            sdv.inc(ins)
        blk.vector(dve_mask)
        mask_done = sdv.n
        self.dbg("msk0", SCRB[:, 0:128], self.sdv)
        self.dbg("msk7", SCRB[:, 896:1024], self.sdv)
        es_qk.close()

        es_epi = ExitStack()
        p2 = es_epi.enter_context(nc.psum_tensor("p2", [16, 128], F32))

        def pe_e16(tensor):
            tensor.wait_ge(sdv.h, mask_done)
            for dg in range(8):
                ins = tensor.matmul(p2[:], self.E16, masked(dg),
                                    start=(dg == 0), stop=(dg == 7))
            smm.inc(ins)
        blk.tensor(pe_e16)
        e16_done = smm.n

        EN_P = SM[0:16, 128:144]

        def dve_ered(vector):
            vector.wait_ge(smm.h, e16_done)
            s1 = SM[0:16, 208:272]
            s2 = SM[0:16, 272:304]
            s0 = SM[0:16, 240:368][:, 0:128]
            sdv.inc(vector.tensor_copy(s0, p2.ap()))
            sdv.wait(vector)
            sdv.inc(vector.tensor_add(s1, s0[:, 0:127:2], s0[:, 1:128:2]))
            sdv.wait(vector)
            sdv.inc(vector.tensor_add(s2, s1[:, 0:63:2], s1[:, 1:64:2]))
            sdv.wait(vector)
            sdv.inc(vector.tensor_add(EN_P, s2[:, 0:31:2], s2[:, 1:32:2]))
        blk.vector(dve_ered)
        ered_done = sdv.n

        def en_ar_w(sync):
            sync.wait_ge(sdv.h, ered_done)
            sync.wait_ge(sq.h, sq.n)
            sq.inc(sync.dma_start(out=self.d_en_in[:], in_=EN_P), 16)
        blk.sync(en_ar_w)
        en_bounce = sq.n

        def en_cc(gpsimd):
            gpsimd.wait_ge(sq.h, en_bounce)
            self.scc2.inc(gpsimd.collective_compute(
                "AllReduce", ALU.add, replica_groups=[list(range(NCORES))],
                ins=[self.d_en_in[:]], outs=[self.d_en_out[:]]))
        blk.gpsimd(en_cc)
        en_ar_done = self.scc2.n

        EN = SM[0:16, 144:160]

        def en_back(sync):
            sync.wait_ge(self.scc2.h, en_ar_done)
            sync.wait_ge(sq.h, sq.n)
            sq.inc(sync.dma_start(out=EN, in_=self.d_en_out[:]), 16)
        blk.sync(en_back)
        en_back_done = sq.n

        ATT = SM[0:16, 160:176]
        att_bf = SCRB[0:16, 4096:4112]

        def sm1(vector):
            vector.wait_ge(sq.h, en_back_done)
            nm = SM[0:16, 176:177]
            sdv.inc(vector.tensor_reduce(nm, EN, AX.X, ALU.max, negate=True))
            sdv.wait(vector)
            sdv.inc(vector.tensor_scalar_mul(SM[0:16, 177:178], nm, 0.125))
        blk.vector(sm1)

        def sm2(scalar):
            sdv.wait(scalar)
            sac.inc(scalar.activation(ATT, EN, AF.Exp,
                                      bias=SM[0:16, 177:178],
                                      scale=np.float32(0.125),
                                      accum_out=SM[0:16, 178:179]))
        blk.scalar(sm2)

        def sm3(vector):
            sac.wait(vector)
            sdv.inc(vector.reciprocal(SM[0:16, 179:180], SM[0:16, 178:179]))
            sdv.wait(vector)
            sdv.inc(vector.tensor_scalar_mul(ATT, ATT, SM[0:16, 179:180]))
            sdv.wait(vector)
            sdv.inc(vector.tensor_copy(att_bf, ATT))
        blk.vector(sm3)
        att_done = sdv.n
        self.dbg("energy", EN, self.sq)
        self.dbg("en_p", EN_P, self.sdv)
        self.dbg("att", ATT, self.sdv)

        def ld_wvt(gpsimd):
            gpsimd.wait_ge(smm.h, en_d[TC - 1])
            self.lw_inline(gpsimd, self.wi["wv0T"], WS1.ap(), D, D)
        blk.gpsimd(ld_wvt)
        wvt_ld = self.sg.n

        def ld_wo(gpsimd):
            self.lw_inline(gpsimd, self.wi["wo0"], WS2.ap(), D, D)
        blk.gpsimd(ld_wo)
        wo_ld = self.sg.n

        ph = es_epi.enter_context(nc.psum_tensor("ph", [16, 2], F32))
        px = es_epi.enter_context(nc.psum_tensor("px", [128, 16], F32))
        attm = SCRB[0:16, 4112:4114]
        Xall = SM[:, 192:208]

        def pe_attm(tensor):
            tensor.wait_ge(sdv.h, att_done)
            smm.inc(tensor.matmul(ph[:], att_bf, self.HS[:], start=True, stop=True))
        blk.tensor(pe_attm)
        attm_mm = smm.n

        def dv_attm(vector):
            vector.wait_ge(smm.h, attm_mm)
            sdv.inc(vector.tensor_copy(attm, ph[:]))
        blk.vector(dv_attm)
        attm_done = sdv.n

        def pe_x(tensor):
            tensor.wait_ge(sdv.h, attm_done)
            for p in range(KC):
                ins = tensor.matmul(px[:, 2 * p:2 * p + 2],
                                    self.SALL[:, 128 * p:128 * (p + 1)], attm,
                                    start=True, stop=True)
            smm.inc(ins)
        blk.tensor(pe_x)
        x_mm = smm.n

        def dv_x(vector):
            vector.wait_ge(smm.h, x_mm)
            sdv.inc(vector.tensor_copy(Xall, px[:]))
        blk.vector(dv_x)
        xall_done = sdv.n

        E_sb = SCRB[:, 0:1024]
        fold_sb = SCRB[:, 1024:2048]

        def dv_ebuild(vector):
            vector.wait_ge(sdv.h, xall_done)
            for p in range(KC):
                for A in range(2):
                    ins = vector.tensor_scalar_mul(
                        E_sb[:, 128 * p + 64 * A:128 * p + 64 * A + 64],
                        self.MASK128[:, 64 * A:64 * A + 64],
                        Xall[:, 2 * p + A:2 * p + A + 1])
            sdv.inc(ins)
        blk.vector(dv_ebuild)
        ebuild_done = sdv.n

        pf = es_epi.enter_context(nc.psum_tensor("pf", [128, 128], F32))
        fold_mm = [0] * KC
        fold_ev = [0] * KC
        for i in range(KC):
            def pe_fold_i(tensor, i=i):
                if i == 0:
                    tensor.wait_ge(sdv.h, ebuild_done)
                    tensor.wait_ge(sg.h, wo_ld)
                else:
                    tensor.wait_ge(sdv.h, fold_ev[i - 1])
                for p in range(KC):
                    ins = tensor.matmul(
                        pf[:], WS1[:, 1024 * p + 128 * i:1024 * p + 128 * (i + 1)],
                        E_sb[:, 128 * p:128 * (p + 1)],
                        start=(p == 0), stop=(p == KC - 1))
                fold_mm[i] = smm.inc(ins)
            blk.tensor(pe_fold_i)

            def dv_fold_i(vector, i=i):
                vector.wait_ge(smm.h, fold_mm[i])
                fold_ev[i] = sdv.inc(vector.tensor_copy(
                    fold_sb[:, 128 * i:128 * (i + 1)], pf[:]))
            blk.vector(dv_fold_i)
        fold_done = sdv.n
        es_epi.close()

        SCRF = self.SCRF
        wsl = SCRF[:, 1024:2048]

        def slab_ld(sync):
            sync.wait_ge(sq.h, sq.n)
            sq.inc(sync.dma_start(out=wsl, in_=self.wi["wfslab"][:]), 16)
        blk.sync(slab_ld)
        slab_ld_d = sq.n

        w_eff = SCRB[:, 2048:3072]

        def slab_quant(vector):
            vector.wait_ge(sq.h, slab_ld_d)
            vector.wait_ge(smm.h, en_d[TC - 1])
            aw = SM[:, 192:193]
            sdv.inc(vector.tensor_reduce(aw, wsl, AX.X, ALU.max,
                                         apply_absolute_value=True))
            sdv.wait(vector)
            sdv.inc(vector.tensor_scalar_max(aw, aw, 1e-8))
            sdv.wait(vector)
            rw = SM[:, 193:194]
            sdv.inc(vector.reciprocal(rw, aw))
            sdv.wait(vector)
            sdv.inc(vector.tensor_scalar_mul(rw, rw, 127.0))
            sdv.wait(vector)
            sw = SM[:, 194:195]
            sdv.inc(vector.tensor_scalar_mul(sw, aw, 1.0 / 127.0))
            sdv.wait(vector)
            sdv.inc(vector.tensor_scalar_mul(wsl, wsl, rw))
            sdv.wait(vector)
            sdv.inc(vector.tensor_scalar(wsl, wsl, MAGIC, MAGIC,
                                         ALU.add, ALU.subtract))
            sdv.wait(vector)
            sdv.inc(vector.tensor_scalar_mul(w_eff, wsl, sw))
        blk.vector(slab_quant)
        slab_q_d = sdv.n

        wT = SCRB[:, 3072:4096]
        es_sl = ExitStack()
        psl = es_sl.enter_context(nc.psum_tensor("pslab", [128, 512], BF16))
        sl_mm = [0, 0]
        sl_ev = [0, 0]
        for g in range(2):
            def sl_tp(tensor, g=g):
                if g == 0:
                    tensor.wait_ge(sdv.h, slab_q_d)
                else:
                    tensor.wait_ge(sdv.h, sl_ev[g - 1])
                for j in range(4):
                    k = 4 * g + j
                    ins = tensor.transpose(psl[:, 128 * j:128 * (j + 1)],
                                           w_eff[:, 128 * k:128 * (k + 1)],
                                           self.IDB[:])
                sl_mm[g] = smm.inc(ins)
            blk.tensor(sl_tp)

            def sl_ev_f(vector, g=g):
                vector.wait_ge(smm.h, sl_mm[g])
                sl_ev[g] = sdv.inc(vector.tensor_copy(
                    wT[:, 512 * g:512 * (g + 1)], psl[:]))
            blk.vector(sl_ev_f)
        es_sl.close()
        slab_t_d = sdv.n

        def wf_bounce(sync):
            sync.wait_ge(sdv.h, slab_t_d)
            sync.wait_ge(sq.h, sq.n)
            sq.inc(sync.dma_start(
                out=self.d_wf_in.rearrange("k (p d) -> p k d", p=128),
                in_=wT.rearrange("p (k d) -> p k d", d=128)), 16)
        blk.sync(wf_bounce)
        wf_b_d = sq.n

        def wf_cc(gpsimd):
            gpsimd.wait_ge(sq.h, wf_b_d)
            self.scc3.inc(gpsimd.collective_compute(
                "AllGather", ALU.bypass, replica_groups=[list(range(NCORES))],
                ins=[self.d_wf_in[:]], outs=[self.d_wf_out[:]]))
        blk.gpsimd(wf_cc)
        wf_cc_d = self.scc3.n

        def wf_back(gpsimd):
            gpsimd.wait_ge(self.scc3.h, wf_cc_d)
            gpsimd.wait_ge(sdv.h, fold_done)
            gpsimd.wait_ge(sg.h, sg.n)
            srcall = self.d_wf_out.rearrange("(r k) (p d) -> k p r d", r=NCORES, p=128)
            for k in range(KC):
                dst = WS1[:, 1024 * k:1024 * (k + 1)].rearrange(
                    "p (r d) -> p r d", d=128)
                sg.inc(gpsimd.dma_start(out=dst, in_=srcall[k]), 16)
        blk.gpsimd(wf_back)
        self.wf_back_d = sg.n



        es_rt = ExitStack()
        pvm = [es_rt.enter_context(nc.psum_tensor(f"pvm{i}", [128, 512], F32))
               for i in range(2)]
        pz = [es_rt.enter_context(nc.psum_tensor(f"pz{i}", [128, 512], F32))
              for i in range(2)]

        SB_STRM = [0, 4096]
        SB_WIN = [8192, 10240]
        SB_ZW = [[12288, 13312], [14336, 15360]]

        strm_d = [0] * 32
        vm_mm = [0] * 32
        vm_ev = [0] * 32
        dup_d = [0] * 8
        z_mm = {}
        z_ev = {}
        ln0out_r = self.d_ln0_out.rearrange("(r k p) t -> p r k t", p=128, k=KC)

        def emit_strm(n):
            par = n % 2

            def strm(eng, n=n, par=par):
                if n == 0:
                    eng.wait_ge(self.scc1.h, ag_ln0_done)
                if n == 1:
                    eng.wait_ge(sdv.h, fold_done)
                if n >= 2:
                    eng.wait_ge(smm.h, vm_mm[n - 2])
                R, c0 = n // 4, (n % 4) * 512
                dst = UTM[:, SB_STRM[par]:SB_STRM[par] + 4096].rearrange(
                    "p (k c) -> p k c", c=512)
                strm_d[n] = self.sqP[par].inc(eng.dma_start(
                    out=dst, in_=ln0out_r[:, R, :, c0:c0 + 512]), 16)
            (blk.sync if par == 0 else blk.scalar)(strm)

        def emit_vm(n):
            par = n % 2
            rt, nj = n // 4, n % 4
            wpar = rt % 2

            def pe_vm(tensor, n=n, par=par):
                if n == 0:
                    tensor.wait_ge(sdv.h, fold_done)
                tensor.wait_ge(self.sqP[par].h, strm_d[n])
                if n >= 2:
                    tensor.wait_ge(sdv.h, vm_ev[n - 2])
                buf = UTM[:, SB_STRM[par]:SB_STRM[par] + 4096]
                for i in range(KC):
                    ins = tensor.matmul(pvm[par][:],
                                        fold_sb[:, 128 * i:128 * (i + 1)],
                                        buf[:, 512 * i:512 * (i + 1)],
                                        start=(i == 0), stop=(i == KC - 1))
                vm_mm[n] = smm.inc(ins)
            blk.tensor(pe_vm)

            def dv_vm(vector, n=n, par=par, rt=rt, nj=nj, wpar=wpar):
                vector.wait_ge(smm.h, vm_mm[n])
                if nj == 0 and rt >= 2:
                    vector.wait_ge(sg.h, dup_d[rt - 2])
                src = pvm[par].ap().rearrange("p (r2 j) -> p j r2", j=16)
                win = UTM[:, SB_WIN[wpar]:SB_WIN[wpar] + 2048]
                dst2 = win.rearrange("p (j r) -> p j r", j=16)[:, :, 32 * nj:32 * nj + 32]
                vm_ev[n] = sdv.inc(vector.tensor_copy(dst2, src))
            blk.vector(dv_vm)

        def emit_dup(rt):
            wpar = rt % 2

            def dup(gpsimd, rt=rt, wpar=wpar):
                gpsimd.wait_ge(sdv.h, vm_ev[4 * rt + 3])
                if rt == 0:
                    gpsimd.wait_ge(sg.h, self.wf_back_d)
                else:
                    gpsimd.wait_ge(sg.h, dup_d[rt - 1])
                win = UTM[:, SB_WIN[wpar]:SB_WIN[wpar] + 2048]
                for A in range(2):
                    if rt >= 2:
                        gpsimd.wait_ge(smm.h, z_mm[(rt - 2, A, 1)])
                    zw = UTM[:, SB_ZW[wpar][A]:SB_ZW[wpar][A] + 1024]
                    srcv = win[64 * A:64 * A + 64, :].rearrange("p (j r) -> p j r", j=16)
                    sg.inc(gpsimd.dma_start(
                        out=zw[0:64, :].rearrange("p (j r) -> p j r", j=8),
                        in_=srcv[:, 0:16:2, :]), 16)
                    sg.inc(gpsimd.dma_start(
                        out=zw[64:128, :].rearrange("p (j r) -> p j r", j=8),
                        in_=srcv[:, 1:16:2, :]), 16)
                dup_d[rt] = sg.n
            blk.gpsimd(dup)

        def emit_z(rt):
            wpar = rt % 2
            for A in range(2):
                for n2 in range(2):
                    g = rt * 4 + A * 2 + n2
                    par = g % 2
                    prev_key = (((g - 2) // 4, ((g - 2) % 4) // 2, (g - 2) % 2)
                                if g >= 2 else None)

                    def pe_z(tensor, rt=rt, A=A, n2=n2, par=par,
                             prev_key=prev_key, wpar=wpar):
                        tensor.wait_ge(sg.h, dup_d[rt])
                        if rt == 0 and A == 0 and n2 == 0:
                            tensor.wait_ge(sg.h, wo_ld)
                        if prev_key is not None:
                            tensor.wait_ge(sdv.h, z_ev[prev_key])
                        zw = UTM[:, SB_ZW[wpar][A]:SB_ZW[wpar][A] + 1024]
                        for jj in range(KC):
                            tensor.matmul(
                                pz[par][:], zw[:, 128 * jj:128 * (jj + 1)],
                                WS2[:, 1024 * jj + 512 * n2:1024 * jj + 512 * (n2 + 1)],
                                start=(jj == 0), stop=False)
                        ins = tensor.matmul(pz[par][:], ONEB[:],
                                            BROWS[0:1, 512 * n2:512 * (n2 + 1)],
                                            start=False, stop=True)
                        z_mm[(rt, A, n2)] = smm.inc(ins)
                    blk.tensor(pe_z)

                    def dv_z(vector, rt=rt, A=A, n2=n2, par=par):
                        vector.wait_ge(smm.h, z_mm[(rt, A, n2)])
                        ch = A * 8 + rt
                        xs = self.XM[:, 1024 * ch + 512 * n2:1024 * ch + 512 * (n2 + 1)]
                        z_ev[(rt, A, n2)] = sdv.inc(
                            vector.tensor_add(xs, xs, pz[par][:]))
                    blk.vector(dv_z)
        self.l0z_ev = z_ev

        for rt in range(8):
            for nj in range(4):
                n = 4 * rt + nj
                emit_strm(n)
                emit_vm(n)
            emit_dup(rt)
            if rt >= 1:
                emit_z(rt - 1)
        emit_z(7)

        es_rt.close()
        es_en.close()
        self.x_l0a_done = sdv.n
        self.dbg("x_l0a", self.XM[:], self.sdv)

    def phase_l0_ffn(self):
        nc, blk = self.nc, self.blk
        sq, sg, smm, sdv, sac, scc = (self.sq, self.sg, self.smm, self.sdv,
                                      self.sac, self.scc)
        FMA, UTM, SCRB, SCRF, SM = (self.FMA, self.UTM, self.SCRB,
                                    self.SCRF, self.SM)
        WS1, BROWS = self.WS1, self.BROWS

        stats = SM[:, 0:128]
        cols = self.ln_stats(
            stats, extra_minmax=True,
            chunk_wait=lambda v, t: v.wait_ge(
                sdv.h, self.l0z_ev[(t % 8, t // 8, 1)]))
        CM, CV, CI, CN, CX, CNM = (cols["m"], cols["v"], cols["inv"],
                                   cols["negmi"], cols["xmax"], cols["negxmin"])

        QA, QB, QSX, QRS = 128, 144, 160, 176

        def qstats(vector):
            m_ = stats[:, CM:CM + 16]
            t1 = SM[:, QA:QA + 16]
            t2 = SM[:, QB:QB + 16]
            sdv.inc(vector.tensor_sub(t1, stats[:, CX:CX + 16], m_))
            sdv.wait(vector)
            sdv.inc(vector.tensor_add(t2, stats[:, CNM:CNM + 16], m_))
            sdv.wait(vector)
            amax = SM[:, QSX:QSX + 16]
            sdv.inc(vector.tensor_max(amax, t1, t2))
            sdv.wait(vector)
            sdv.inc(vector.tensor_mul(amax, amax, stats[:, CI:CI + 16]))
            sdv.wait(vector)
            sdv.inc(vector.tensor_scalar_max(amax, amax, 1e-8))
            sdv.wait(vector)
            r127 = SM[:, QRS:QRS + 16]
            sdv.inc(vector.reciprocal(r127, amax))
            sdv.wait(vector)
            sdv.inc(vector.tensor_scalar_mul(r127, r127, 127.0))
            sdv.wait(vector)
            sdv.inc(vector.tensor_mul(SM[:, QA:QA + 16], stats[:, CI:CI + 16], r127))
            sdv.wait(vector)
            sdv.inc(vector.scalar_tensor_tensor(
                SM[:, QB:QB + 16], m_, -1.0, SM[:, QA:QA + 16], ALU.mult, ALU.mult))
            sdv.wait(vector)
            sdv.inc(vector.tensor_scalar_mul(SM[:, QSX:QSX + 16], amax, 1.0 / 127.0))
        blk.vector(qstats)
        qstats_done = sdv.n

        rsx_rows = BROWS[0:1, 3328:3328 + 2048]
        rsx_bf = SCRB[:, 4114:4130]

        def rsx_cast(vector):
            vector.wait_ge(sdv.h, qstats_done)
            sdv.inc(vector.tensor_copy(rsx_bf, SM[:, QRS:QRS + 16]))
        blk.vector(rsx_cast)
        rsx_bf_d = sdv.n

        es_rs = ExitStack()
        prs = es_rs.enter_context(nc.psum_tensor("prs", [1, 128], BF16))
        rs_mm = [0] * TC
        rs_ev = [0] * TC
        for t in range(TC):
            def rs_tp(tensor, t=t):
                if t == 0:
                    tensor.wait_ge(sdv.h, rsx_bf_d)
                else:
                    tensor.wait_ge(sdv.h, rs_ev[t - 1])
                rs_mm[t] = smm.inc(tensor.transpose(
                    prs[:], rsx_bf[:, t:t + 1], self.IDB[:]))
            blk.tensor(rs_tp)

            def rs_ev_f(vector, t=t):
                vector.wait_ge(smm.h, rs_mm[t])
                rs_ev[t] = sdv.inc(vector.tensor_copy(
                    rsx_rows[0:1, 128 * t:128 * (t + 1)], prs[:]))
            blk.vector(rs_ev_f)
        es_rs.close()
        rsx_done = sdv.n

        tq2 = [SCRF[:, 0:1024], SCRF[:, 1024:2048]]
        sacb = sac.n
        xq_s1 = [sacb + t + 1 for t in range(TC)]
        xq_d = [0] * TC
        for t in range(TC):
            def xq_v(vector, t=t):
                vector.wait_ge(sac.h, xq_s1[t])
                xq_d[t] = sdv.inc(vector.tensor_scalar(
                    UTM[:, 1024 * t:1024 * (t + 1)], tq2[t % 2], MAGIC, MAGIC,
                    ALU.add, ALU.subtract))
            blk.vector(xq_v)
        xq_done = sdv.n

        def xq_a_all(scalar):
            for t in range(TC):
                if t == 0:
                    scalar.wait_ge(sdv.h, qstats_done)
                if t >= 2:
                    scalar.wait_ge(sdv.h, xq_d[t - 2])
                v = sac.inc(scalar.activation(
                    tq2[t % 2], self.XM[:, 1024 * t:1024 * (t + 1)], AF.Identity,
                    bias=SM[:, QB + t:QB + t + 1], scale=SM[:, QA + t:QA + t + 1]))
                assert v == xq_s1[t]
        blk.scalar(xq_a_all)

        es_f = ExitStack()
        pfn = [es_f.enter_context(nc.psum_tensor(f"pfn{i}", [128, 1024], F32))
               for i in range(2)]
        xq_tpb = self.transpose_tm_fm(
            "xq", UTM, FMA, wait_sem=sdv, order="t",
            group_wait=lambda t0: xq_d[t0 + 3])
        xqfm_done = sdv.n

        f_mm = [0] * TC
        f_ev = [0] * TC
        for t in range(TC):
            par = t % 2

            def f_pe(tensor, t=t, par=par):
                if t == 0:
                    tensor.wait_ge(sdv.h, rsx_done)
                    tensor.wait_ge(sg.h, self.wf_back_d)
                if t % 4 == 0:
                    tensor.wait_ge(sdv.h, xq_tpb + 8 * (t // 4 + 1))
                if t >= 2:
                    tensor.wait_ge(sdv.h, f_ev[t - 2])
                for n in range(2):
                    for k in range(KC):
                        tensor.matmul(
                            pfn[par][:, 512 * n:512 * (n + 1)],
                            FMA[:, k * T + 128 * t:k * T + 128 * (t + 1)],
                            WS1[:, 1024 * k + 512 * n:1024 * k + 512 * (n + 1)],
                            start=(k == 0), stop=False)
                    ins = tensor.matmul(
                        pfn[par][:, 512 * n:512 * (n + 1)],
                        rsx_rows[0:1, 128 * t:128 * (t + 1)],
                        BROWS[0:1, D + 512 * n:D + 512 * (n + 1)],
                        start=False, stop=True)
                f_mm[t] = smm.inc(ins)
            blk.tensor(f_pe)

            def f_ev_f(vector, t=t, par=par):
                vector.wait_ge(smm.h, f_mm[t])
                xs = self.XM[:, 1024 * t:1024 * (t + 1)]
                f_ev[t] = sdv.inc(vector.scalar_tensor_tensor(
                    xs, pfn[par][:], SM[:, QSX + t:QSX + t + 1], xs,
                    ALU.mult, ALU.add))
            blk.vector(f_ev_f)
        es_f.close()
        self.x_l0f_done = sdv.n
        self.dbg("x_l0f", self.XM[:], self.sdv)

    def phase_l1_attn(self):
        nc, blk = self.nc, self.blk
        sq, sg, smm, sdv, sac, scc = (self.sq, self.sg, self.smm, self.sdv,
                                      self.sac, self.scc)
        FMA, UTM, SCRB, SCRF, SM = (self.FMA, self.UTM, self.SCRB, self.SCRF,
                                    self.SM)
        WS2, BROWS, PP = self.WS2, self.BROWS, self.PP

        def l1w(gpsimd):
            gpsimd.wait_ge(sdv.h, self.x_l0f_done)
            self.lw_inline(gpsimd, self.wi["wq1T"], WS2.ap()[:, 0:1024], D, 128)
            self.lw_inline(gpsimd, self.wi["wk1T"], WS2.ap()[:, 1024:2048], D, 128)
            self.lw_inline(gpsimd, self.wi["wv1T"], WS2.ap()[:, 2048:3072], D, 128)
            self.lw_inline(gpsimd, self.wi["wo1T"], WS2.ap()[:, 3072:4096], 128, 1024)
        blk.gpsimd(l1w)
        l1w_d = sg.n

        def ppv(vector):
            vector.wait_ge(self.sq.h, self.loads_done)
            iw = PP[:, 20:21]
            sdv.inc(vector.tensor_scalar_add(iw, PP[:, 16:17], 1e-8))
            sdv.wait(vector)
            sdv.inc(vector.reciprocal(iw, iw))
            sdv.wait(vector)
            sdv.inc(vector.scalar_tensor_tensor(
                PP[:, 21:22], PP[:, 18:19], iw, PP[:, 17:18], ALU.mult, ALU.add))
            sdv.inc(vector.scalar_tensor_tensor(
                PP[:, 22:23], PP[:, 19:20], iw, PP[:, 17:18], ALU.mult, ALU.add))
        blk.vector(ppv)
        ppv_d = sdv.n

        stats = SM[:, 0:128]
        cols = self.ln_stats(stats,
                             wait=lambda v: v.wait_ge(sdv.h, self.x_l0f_done))
        ln2_apb = self.ln_apply(stats, cols, UTM)
        ln2_tp_marks = [0] * 4

        def emit_ln2_tp(g):
            b = self.transpose_tm_fm(
                f"ln2_{g}", UTM[:, 4096 * g:4096 * (g + 1)], FMA,
                n_tc=4, wait_sem=sac, order="t",
                group_wait=lambda t0, g=g: ln2_apb + 4 * g + 4,
                t_off=4 * g, fm_kstride=2048)
            ln2_tp_marks[g] = b + 8

        es_qkv = ExitStack()
        pth = [es_qkv.enter_context(nc.psum_tensor(f"pth{i}", [128, 512], F32))
               for i in range(2)]
        TH, TT2 = 0, 2048

        def theta_of(side, dst_cos, dst_sin):
            wofs = 1024 * side
            bias_col = PP[:, 21 + side:22 + side]
            mm_marks = [0] * 4
            sacb = sac.n
            ev_marks = [sacb + n + 1 for n in range(4)]
            sin_marks = [sacb + 4 + i + 1 for i in range(4)]
            entry_ac = sacb
            entry_dv = sdv.n
            for n in range(4):
                if side == 0:
                    emit_ln2_tp(n)

                def pj(tensor, n=n, wofs=wofs):
                    if n == 0:
                        tensor.wait_ge(sg.h, l1w_d)
                        tensor.wait_ge(sac.h, entry_ac)
                        tensor.wait_ge(sdv.h, entry_dv)
                    if n >= 2:
                        tensor.wait_ge(sac.h, ev_marks[n - 2])
                    tensor.wait_ge(sdv.h, ln2_tp_marks[n])
                    for k in range(KC):
                        ins = tensor.matmul(
                            pth[n % 2][:],
                            WS2[:, wofs + 128 * k:wofs + 128 * (k + 1)],
                            FMA[:, k * T + 512 * n:k * T + 512 * (n + 1)],
                            start=(k == 0), stop=(k == KC - 1))
                    mm_marks[n] = smm.inc(ins)
                blk.tensor(pj)

            tgv_d = [0] * 4
            for i, (fn, h) in enumerate([("cos", 0), ("cos", 1),
                                         ("sin", 0), ("sin", 1)]):
                def tg_v(vector, fn=fn, h=h, i=i):
                    if i == 0:
                        vector.wait_ge(sac.h, entry_ac)
                    else:
                        vector.wait_ge(sac.h, sin_marks[i - 1])
                    vector.wait_ge(sac.h, ev_marks[2 * h + 1])
                    th = SCRF[:, TH + 1024 * h:TH + 1024 * (h + 1)]
                    tt = SCRF[:, TT2:TT2 + 1024]
                    ins = vector.tensor_scalar(
                        tt, th, INV2PI, 0.25 if fn == "cos" else 0.0,
                        ALU.mult, ALU.add)
                    sdv.inc(ins)
                    sdv.wait(vector)
                    nn = self.SCRF2[:, 0:1024]
                    sdv.inc(vector.tensor_scalar(nn, tt, MAGIC, MAGIC,
                                                 ALU.add, ALU.subtract))
                    sdv.wait(vector)
                    tgv_d[i] = sdv.inc(vector.scalar_tensor_tensor(
                        tt, nn, -TWO_PI, th, ALU.mult, ALU.add))
                blk.vector(tg_v)

            def act_all(scalar, side=side, bias_col=bias_col,
                        dst_cos=dst_cos, dst_sin=dst_sin):
                for n in range(4):
                    scalar.wait_ge(smm.h, mm_marks[n])
                    v = sac.inc(scalar.activation(
                        SCRF[:, TH + 512 * n:TH + 512 * (n + 1)], pth[n % 2][:],
                        AF.Identity, bias=bias_col, scale=PP[:, 20:21]))
                    assert v == ev_marks[n]
                for i, (fn, h) in enumerate([("cos", 0), ("cos", 1),
                                             ("sin", 0), ("sin", 1)]):
                    dst = dst_cos if fn == "cos" else dst_sin
                    scalar.wait_ge(sdv.h, tgv_d[i])
                    bias = np.float32(PI / 2) if fn == "cos" else np.float32(0.0)
                    v = sac.inc(scalar.activation(
                        dst[:, 1024 * h:1024 * (h + 1)],
                        SCRF[:, TT2:TT2 + 1024], AF.Sin, bias=bias,
                        scale=np.float32(1.0)))
                    assert v == sin_marks[i]
            blk.scalar(act_all)
            return sac.n

        self.l1_scrf_free = self.sdv.n
        cosQ = SCRB[:, 0:2048]; sinQ = SCRB[:, 2048:4096]
        q_trig_d = theta_of(0, cosQ, sinQ)
        cosK_l = SCRB[:, 4096:6144]; sinK_l = SCRB[:, 6144:8192]
        k_trig_d = theta_of(1, cosK_l, sinK_l)

        pv = [es_qkv.enter_context(nc.psum_tensor(f"pv{i}", [128, 128], F32))
              for i in range(2)]
        V_tm = UTM[:, 0:2048]
        v_mm = [0] * TC
        v_ev = [0] * TC
        for t in range(TC):
            def vp(tensor, t=t):
                if t == 0:
                    tensor.wait_ge(sg.h, l1w_d)
                if t % 4 == 0:
                    tensor.wait_ge(sdv.h, ln2_tp_marks[t // 4])
                if t >= 2:
                    tensor.wait_ge(sdv.h, v_ev[t - 2])
                for k in range(KC):
                    tensor.matmul(pv[t % 2][:],
                                  FMA[:, k * T + 128 * t:k * T + 128 * (t + 1)],
                                  WS2[:, 2048 + 128 * k:2048 + 128 * (k + 1)],
                                  start=(k == 0), stop=False)
                ins = tensor.matmul(pv[t % 2][:], self.ONEB[:],
                                    BROWS[0:1, 3072:3072 + 128],
                                    start=False, stop=True)
                v_mm[t] = smm.inc(ins)
            blk.tensor(vp)

            def vev(vector, t=t):
                vector.wait_ge(smm.h, v_mm[t])
                v_ev[t] = sdv.inc(vector.tensor_copy(
                    V_tm[:, 128 * t:128 * (t + 1)], pv[t % 2][:]))
            blk.vector(vev)
        v_d = sdv.n

        CKO, SKO, VO = 0, 128 * T, 2 * 128 * T
        kvflat = self.d_kv_in.rearrange("a b -> (a b)")

        def kv_bw(sync):
            sync.wait_ge(sac.h, k_trig_d)
            sync.wait_ge(sq.h, sq.n)
            sq.inc(sync.dma_start(
                out=kvflat[CKO:CKO + 128 * T].rearrange("(p c) -> p c", p=128),
                in_=cosK_l), 16)
            sq.inc(sync.dma_start(
                out=kvflat[SKO:SKO + 128 * T].rearrange("(p c) -> p c", p=128),
                in_=sinK_l), 16)
            sync.wait_ge(sdv.h, v_d)
            dstv = kvflat[VO:VO + T * 128].rearrange(
                "(tt p h) -> p tt h", p=128, h=128)
            sq.inc(sync.dma_start(
                out=dstv, in_=V_tm.rearrange("p (tt h) -> p tt h", h=128)), 16)
        blk.sync(kv_bw)
        kv_bw_d = sq.n

        def kv_cc(gpsimd):
            gpsimd.wait_ge(sq.h, kv_bw_d)
            self.scc4.inc(gpsimd.collective_compute(
                "AllGather", ALU.bypass, replica_groups=[list(range(NCORES))],
                ins=[self.d_kv_in[:]], outs=[self.d_kv_out[:]]))
        blk.gpsimd(kv_cc)
        kv_cc_d = self.scc4.n
        es_qkv.close()

        SB_CK = [2048, 4096]
        SB_SK = [6144, 8192]
        SB_V = [10240, 12352]
        SB_PT = [4096, 5120, 6144, 7168]

        kvr = self.d_kv_out.rearrange("(r a) b -> r (a b)", r=NCORES)

        es_sim = ExitStack()
        pav = [es_sim.enter_context(nc.psum_tensor(f"pav{i}", [128, 512], F32))
               for i in range(3)]
        pavt = es_sim.enter_context(nc.psum_tensor("pavt", [128, 128], BF16))

        def av_slot(qt):
            b, s = divmod(qt, 3)
            return pav[b][:, 129 * s:129 * s + 129]

        kv_strm = {}
        sim_mm = {}
        exp_d = {}
        av_mm = {}
        onesset = [0, 0]

        def ones_pre(vector):
            for par in range(2):
                vb = UTM[:, SB_V[par]:SB_V[par] + 2112]
                sdv.inc(vector.memset(
                    vb.rearrange("p (kl c) -> p kl c", c=132)[:, :, 128:129], 1.0))
        blk.vector(ones_pre)
        ones_d = sdv.n

        self.l1_z_done = [None, None]
        self._pz = {}
        for pa in range(2):
            es_psim = ExitStack()
            psim = [es_psim.enter_context(
                nc.psum_tensor(f"psim{pa}_{i}", [128, 1024], F32))
                for i in range(2)]

            def emit_kvst(R, pa=pa):
                par = R % 2
                sc = pa * NCORES + R

                def kvst(sync, pa=pa, R=R, par=par, sc=sc):
                    if sc == 0:
                        sync.wait_ge(self.scc4.h, kv_cc_d)
                        sync.wait_ge(sdv.h, ones_d)
                    if sc == 1:
                        sync.wait_ge(smm.h, sim_mm[(0, 0)])
                        sync.wait_ge(sdv.h, ones_d)
                    prev = sc - 2
                    if prev >= 0:
                        pR = prev % NCORES
                        sync.wait_ge(smm.h, av_mm[(prev // NCORES, pR, 15)])
                    blk_base = kvr[R]
                    sp = self.sqP[par]
                    sp.inc(sync.dma_start(
                        out=UTM[:, SB_CK[par]:SB_CK[par] + 2048],
                        in_=blk_base[CKO:CKO + 128 * T].rearrange(
                            "(p c) -> p c", p=128)), 16)
                    sp.inc(sync.dma_start(
                        out=UTM[:, SB_SK[par]:SB_SK[par] + 2048],
                        in_=blk_base[SKO:SKO + 128 * T].rearrange(
                            "(p c) -> p c", p=128)), 16)
                    vb = UTM[:, SB_V[par]:SB_V[par] + 2112].rearrange(
                        "p (kl c) -> p kl c", c=132)[:, :, 0:128]
                    sp.inc(sync.dma_start(
                        out=vb,
                        in_=blk_base[VO:VO + T * 128].rearrange(
                            "(kl p h) -> p kl h", p=128, h=128)), 16)
                    kv_strm[sc] = sp.n
                blk.sync(kvst)

            def emit_sim(it, pa=pa, psim=psim):
                R, kl = divmod(it, 16)
                par, kpar, sc = R % 2, it % 2, pa * NCORES + R

                def sim_f(tensor, R=R, kl=kl, par=par, kpar=kpar, sc=sc, it=it):
                    if kl == 0:
                        tensor.wait_ge(self.sqP[par].h, kv_strm[sc])
                    if pa == 1 and it == 0:
                        tensor.wait_ge(sac.h, exp_d[(0, 127)])
                    if it >= 2:
                        tensor.wait_ge(sac.h, exp_d[(pa, it - 2)])
                    ck = UTM[:, SB_CK[par] + 128 * kl:SB_CK[par] + 128 * (kl + 1)]
                    sk = UTM[:, SB_SK[par] + 128 * kl:SB_SK[par] + 128 * (kl + 1)]
                    for n in range(2):
                        tensor.matmul(psim[kpar][:, 512 * n:512 * (n + 1)],
                                      ck, cosQ[:, 1024 * pa + 512 * n:
                                               1024 * pa + 512 * (n + 1)],
                                      start=True, stop=False)
                        ins = tensor.matmul(
                            psim[kpar][:, 512 * n:512 * (n + 1)],
                            sk, sinQ[:, 1024 * pa + 512 * n:
                                     1024 * pa + 512 * (n + 1)],
                            start=False, stop=True)
                    sim_mm[(sc, kl)] = smm.inc(ins)
                blk.tensor(sim_f)

            sac_base = sac.n
            for it_ in range(128):
                exp_d[(pa, it_)] = sac_base + it_ + 1

            def exp_all(scalar, pa=pa, psim=psim):
                for it in range(128):
                    R, kl = divmod(it, 16)
                    kpar, sc = it % 2, pa * NCORES + R
                    ptj = it % 4
                    scalar.wait_ge(smm.h, sim_mm[(sc, kl)])
                    if it >= 4:
                        g2 = it - 4
                        scalar.wait_ge(smm.h, av_mm[(pa, g2 // 16, g2 % 16)])
                    v = sac.inc(scalar.activation(
                        SCRB[:, SB_PT[ptj]:SB_PT[ptj] + 1024], psim[kpar][:],
                        AF.Exp, bias=np.float32(-CEXP), scale=np.float32(RS128)))
                    assert v == exp_d[(pa, it)]

            def emit_av(it, pa=pa):
                R, kl = divmod(it, 16)
                par, kpar = R % 2, it % 2

                def av_f(tensor, R=R, kl=kl, kpar=kpar, par=par, it=it):
                    tensor.wait_ge(sac.h, exp_d[(pa, it)])
                    if pa == 1 and it == 0:
                        tensor.wait_ge(sdv.h, self._pz[0]["avev_done"])
                    pt = SCRB[:, SB_PT[it % 4]:SB_PT[it % 4] + 1024]
                    vb = UTM[:, SB_V[par] + 132 * kl:SB_V[par] + 132 * kl + 129]
                    first = (R == 0 and kl == 0)
                    last = (R == NCORES - 1 and kl == 15)
                    for qt in range(8):
                        ins = tensor.matmul(
                            av_slot(qt), pt[:, 128 * qt:128 * (qt + 1)], vb,
                            start=(first and qt % 3 == 0),
                            stop=(last and qt in (2, 5, 7)))
                    av_mm[(pa, R, kl)] = smm.inc(ins)
                blk.tensor(av_f)

            emit_kvst(0)
            emit_sim(0)
            for it in range(128):
                if it + 1 < 128:
                    if (it + 1) % 16 == 0:
                        emit_kvst((it + 1) // 16)
                    emit_sim(it + 1)
                    if pa == 1 and thunks0:
                        thunks0.pop(0)()
                emit_av(it)
            blk.scalar(exp_all)

            es_psim.close()
            thunks0 = self._l1_pass_evac(pa, av_slot, av_mm, pavt)
            if pa == 1:
                for t_ in thunks0:
                    t_()
                thunks0 = []

        self._l1_tail_z()
        es_sim.close()
        self.x_l1a_done = sdv.n
        self.dbg("x_l1a", self.XM[:], self.sdv)

    def _l1_pass_evac(self, pa, av_slot, av_mm, pavt):
        nc, blk = self.nc, self.blk
        smm, sdv = self.smm, self.sdv
        SCRB, SM, BROWS = self.SCRB, self.SM, self.BROWS
        if pa == 0:
            av_sb = lambda qt: self.UTM[:, 1024 + 128 * qt:1024 + 128 * (qt + 1)]
            avT_sb = lambda qt: self.UTM[:, 14464 + 128 * qt:14464 + 128 * (qt + 1)]
        else:
            av_sb = lambda qt: SCRB[:, 6144 + 128 * qt:6144 + 128 * (qt + 1)]
            avT_sb = lambda qt: SCRB[:, 7168 + 128 * qt:7168 + 128 * (qt + 1)]
        lcol = SM[:, 128 + 16 * pa:128 + 16 * pa + 8]
        rcol = SM[:, 160 + 16 * pa:160 + 16 * pa + 8]
        lbf = self.UTM[:, 8 * pa:8 * (pa + 1)]
        lrow = BROWS[0:1, 3328 + 1024 * pa:3328 + 1024 * pa + 1024]
        avdone = av_mm[(pa, NCORES - 1, 15)]

        av_ev = [0] * 8
        for qt in range(8):
            def avev(vector, qt=qt):
                if qt == 0:
                    vector.wait_ge(smm.h, avdone)
                    if pa == 1:
                        vector.wait_ge(smm.h, self._pz[0]["avt_mm"][7])
                sl = av_slot(qt)
                sdv.inc(vector.tensor_copy(av_sb(qt), sl[:, 0:128]))
                sdv.wait(vector)
                sdv.inc(vector.tensor_copy(lcol[:, qt:qt + 1], sl[:, 128:129]))
                sdv.wait(vector)
                av_ev[qt] = sdv.inc(vector.reciprocal(rcol[:, qt:qt + 1],
                                                      lcol[:, qt:qt + 1]))
            blk.vector(avev)
        avev_done = sdv.n

        def lbf_cast(vector):
            sdv.inc(vector.tensor_copy(lbf, lcol))
        blk.vector(lbf_cast)
        lbf_d = sdv.n

        st = dict(av_ev=av_ev, avev_done=avev_done, avt_mm=[0] * 8,
                  avt_ev=[0] * 8, lr_mm=[0] * 8, lr_ev=[0] * 8,
                  rcol=rcol, lrow=lrow, avT_sb=avT_sb)
        self._pz[pa] = st
        plr = pavt[0:1, 0:128]

        thunks = []
        for qt in range(8):
            def em_avt(qt=qt):
                def avt(tensor, qt=qt):
                    tensor.wait_ge(sdv.h, st["av_ev"][qt])
                    if qt >= 1:
                        tensor.wait_ge(sdv.h, st["lr_ev"][qt - 1])
                    elif pa == 1:
                        tensor.wait_ge(sdv.h, self._pz[0]["lr_ev"][7])
                    st["avt_mm"][qt] = smm.inc(tensor.transpose(
                        pavt[:], av_sb(qt), self.IDB[:]))
                blk.tensor(avt)
            thunks.append(em_avt)

            def em_avtev(qt=qt):
                def avtev(vector, qt=qt):
                    vector.wait_ge(smm.h, st["avt_mm"][qt])
                    st["avt_ev"][qt] = sdv.inc(
                        vector.tensor_copy(avT_sb(qt), pavt[:]))
                blk.vector(avtev)
            thunks.append(em_avtev)

            def em_lrt(qt=qt):
                def lrt(tensor, qt=qt):
                    tensor.wait_ge(sdv.h, lbf_d)
                    tensor.wait_ge(sdv.h, st["avt_ev"][qt])
                    st["lr_mm"][qt] = smm.inc(tensor.transpose(
                        plr, lbf[:, qt:qt + 1], self.IDB[:]))
                blk.tensor(lrt)
            thunks.append(em_lrt)

            def em_lrtev(qt=qt):
                def lrtev(vector, qt=qt):
                    vector.wait_ge(smm.h, st["lr_mm"][qt])
                    st["lr_ev"][qt] = sdv.inc(vector.tensor_copy(
                        lrow[0:1, 128 * qt:128 * (qt + 1)], plr))
                blk.vector(lrtev)
            thunks.append(em_lrtev)
        return thunks

    def _l1_tail_z(self):
        nc, blk = self.nc, self.blk
        smm, sdv = self.smm, self.sdv
        WS2, BROWS = self.WS2, self.BROWS
        es_z = ExitStack()
        pzz = [es_z.enter_context(nc.psum_tensor(f"pzz{i}", [128, 1024], F32))
               for i in range(2)]
        z_mm = {}
        z_ev = {}
        for g in range(16):
            pa, qt = divmod(g, 8)
            zpar = g % 2
            st = self._pz[pa]

            def zf(tensor, pa=pa, qt=qt, zpar=zpar, g=g, st=st):
                tensor.wait_ge(sdv.h, st["avt_ev"][qt])
                tensor.wait_ge(sdv.h, st["lr_ev"][qt])
                if g >= 2:
                    tensor.wait_ge(sdv.h, z_ev[g - 2])
                for n in range(2):
                    tensor.matmul(pzz[zpar][:, 512 * n:512 * (n + 1)],
                                  st["avT_sb"](qt),
                                  WS2[:, 3072 + 512 * n:3072 + 512 * (n + 1)],
                                  start=True, stop=False)
                    ins = tensor.matmul(
                        pzz[zpar][:, 512 * n:512 * (n + 1)],
                        st["lrow"][0:1, 128 * qt:128 * (qt + 1)],
                        BROWS[0:1, 2048 + 512 * n:2048 + 512 * (n + 1)],
                        start=False, stop=True)
                z_mm[g] = smm.inc(ins)
            blk.tensor(zf)

            def zev(vector, pa=pa, qt=qt, zpar=zpar, g=g, st=st):
                vector.wait_ge(smm.h, z_mm[g])
                t_g = 8 * pa + qt
                xs = self.XM[:, 1024 * t_g:1024 * (t_g + 1)]
                z_ev[g] = sdv.inc(vector.scalar_tensor_tensor(
                    xs, pzz[zpar][:], st["rcol"][:, qt:qt + 1], xs,
                    ALU.mult, ALU.add))
            blk.vector(zev)
        es_z.close()
        self.l1z_ev = z_ev
        self.l1_z_done = [None, sdv.n]

    def phase_l1_ffn(self):
        nc, blk = self.nc, self.blk
        sq, sg, smm, sdv, sac = self.sq, self.sg, self.smm, self.sdv, self.sac
        FMA, UTM, SCRB, SCRF, SM, PP = (self.FMA, self.UTM, self.SCRB,
                                        self.SCRF, self.SM, self.PP)
        WS1, WS2 = self.WS1, self.WS2

        stats = SM[:, 0:128]
        cols = self.ln_stats(
            stats, chunk_wait=lambda v, t: v.wait_ge(sdv.h, self.l1z_ev[t]))
        ln3_stats_d = sdv.n

        def halo_ap(scalar):
            sdv.wait(scalar)
            self.sac.inc(scalar.activation(
                UTM[:, 0:1024], self.XM[:, 1024 * 15:1024 * 16], AF.Identity,
                bias=stats[:, cols["negmi"] + 15:cols["negmi"] + 16],
                scale=stats[:, cols["inv"] + 15:cols["inv"] + 16]))
        blk.scalar(halo_ap)
        halo_ap_d = sac.n

        def halo_bw(sync):
            sync.wait_ge(sac.h, halo_ap_d)
            sync.wait_ge(sq.h, sq.n)
            sq.inc(sync.dma_start(out=self.d_halo_in[:], in_=UTM[127:128, 0:1024]), 16)
        blk.sync(halo_bw)
        halo_bw_d = sq.n

        def halo_cc(gpsimd):
            gpsimd.wait_ge(sq.h, halo_bw_d)
            self.scc.inc(gpsimd.collective_compute(
                "AllGather", ALU.bypass, replica_groups=[list(range(NCORES))],
                ins=[self.d_halo_in[:]], outs=[self.d_halo_out[:]]))
        blk.gpsimd(halo_cc)
        halo_cc_d = self.scc.n

        hg = SCRB[0:8, 0:1024]

        def halo_rd(sync):
            sync.wait_ge(self.scc.h, halo_cc_d)
            sync.wait_ge(sq.h, sq.n)
            sq.inc(sync.dma_start(out=hg, in_=self.d_halo_out[:]), 16)
        blk.sync(halo_rd)
        halo_rd_d = sq.n

        es_h = ExitStack()
        phl = es_h.enter_context(nc.psum_tensor("phl", [128, 1], F32))
        hl_mm = [0] * KC
        hl_ev = [0] * KC
        for k in range(KC):
            def hlt(tensor, k=k):
                if k == 0:
                    tensor.wait_ge(sq.h, halo_rd_d)
                else:
                    tensor.wait_ge(sdv.h, hl_ev[k - 1])
                hl_mm[k] = smm.inc(tensor.matmul(
                    phl[:], hg[:, 128 * k:128 * (k + 1)], self.HSEL[:],
                    start=True, stop=True))
            blk.tensor(hlt)

            def hlev(vector, k=k):
                vector.wait_ge(smm.h, hl_mm[k])
                hl_ev[k] = sdv.inc(vector.tensor_copy(PP[:, 24 + k:25 + k], phl[:]))
            blk.vector(hlev)
        es_h.close()
        halo_done = sdv.n

        wr_ld = [0, 0]
        wk_ld = [0, 0]
        wv_ld = [0, 0]
        h_done = [0, 0]
        for H in range(2):
            t0 = 8 * H

            if H == 0:
                def ldw(gpsimd):
                    gpsimd.wait_ge(sg.h, sg.n)
                    gpsimd.wait_ge(sdv.h, self.x_l1a_done)
                    self.lw_inline(gpsimd, self.wi["wr1T"], WS1.ap(), D, D)
                    self.lw_inline(gpsimd, self.wi["wkf1T"], WS2.ap(), D, D)
                    wr_ld[0] = wk_ld[0] = sg.n
                blk.gpsimd(ldw)
            else:
                wr_ld[1] = wr_ld[0]
                wk_ld[1] = wk_ld[0]

            self.ln_apply(stats, cols, UTM[:, 0:8192], n0=t0, n1=t0 + 8)
            ap_d = sac.n

            self.transpose_tm_fm(f"u{H}", UTM[:, 0:8192], FMA[:, 0:8192],
                                 n_tc=8, wait_sem=sac, wait_val=ap_d)
            u_d = sdv.n

            xr = UTM[:, 8192:16384]
            xk = UTM[:, 0:8192]
            dsc = SCRF[:, 0:1024]
            mix_d = [0] * KC
            for k in range(KC):
                def mix(vector, k=k, H=H):
                    if k == 0:
                        vector.wait_ge(sdv.h, u_d)
                        if H == 0:
                            vector.wait_ge(sdv.h, halo_done)
                    else:
                        vector.wait_ge(sdv.h, mix_d[k - 1])
                    u = FMA[:, 1024 * k:1024 * (k + 1)]
                    sdv.inc(vector.tensor_sub(dsc[:, 1:1024], u[:, 1:1024],
                                              u[:, 0:1023]))
                    sdv.inc(vector.tensor_sub(dsc[:, 0:1], u[:, 0:1],
                                              PP[:, 24 + k:25 + k]))
                    sdv.wait(vector)
                    sdv.inc(vector.scalar_tensor_tensor(
                        xr[:, 1024 * k + 1:1024 * (k + 1)], dsc[:, 1:1024],
                        PP[:, k:k + 1], u[:, 0:1023], ALU.mult, ALU.add))
                    sdv.inc(vector.scalar_tensor_tensor(
                        xr[:, 1024 * k:1024 * k + 1], dsc[:, 0:1],
                        PP[:, k:k + 1], PP[:, 24 + k:25 + k], ALU.mult, ALU.add))
                    sdv.inc(vector.scalar_tensor_tensor(
                        xk[:, 1024 * k + 1:1024 * (k + 1)], dsc[:, 1:1024],
                        PP[:, 8 + k:9 + k], u[:, 0:1023], ALU.mult, ALU.add))
                    ins = vector.scalar_tensor_tensor(
                        xk[:, 1024 * k:1024 * k + 1], dsc[:, 0:1],
                        PP[:, 8 + k:9 + k], PP[:, 24 + k:25 + k], ALU.mult, ALU.add)
                    mix_d[k] = sdv.inc(ins)
                blk.vector(mix)
            mixes_d = sdv.n

            if H == 0:
                def h2halo(vector):
                    sdv.wait(vector)
                    for k in range(KC):
                        sdv.inc(vector.tensor_copy(
                            PP[:, 24 + k:25 + k],
                            FMA[:, 1024 * k + 1023:1024 * (k + 1)]))
                blk.vector(h2halo)
            h2halo_d = sdv.n

            def ldv(gpsimd, H=H, h2halo_d=h2halo_d):
                gpsimd.wait_ge(sg.h, sg.n)
                gpsimd.wait_ge(sdv.h, h2halo_d)
                self.lw_inline(gpsimd, self.wi["wvf1T"],
                               FMA[:, 0:8192], D, D)
                wv_ld[H] = sg.n
            blk.gpsimd(ldv)

            es_r = ExitStack()
            pr = [es_r.enter_context(nc.psum_tensor(f"pr{H}{i}", [128, 1024], F32))
                  for i in range(2)]
            r_mm = [0] * 8
            sacb_r = sac.n
            r_ev = [sacb_r + t + 1 for t in range(8)]
            for t in range(8):
                def rp(tensor, t=t, H=H):
                    if t == 0:
                        tensor.wait_ge(sdv.h, mixes_d)
                        tensor.wait_ge(sg.h, wr_ld[H])
                    if t >= 2:
                        tensor.wait_ge(sac.h, r_ev[t - 2])
                    for n in range(2):
                        for k in range(KC):
                            ins = tensor.matmul(
                                pr[t % 2][:, 512 * n:512 * (n + 1)],
                                xr[:, 1024 * k + 128 * t:1024 * k + 128 * (t + 1)],
                                WS1[:, 1024 * k + 512 * n:1024 * k + 512 * (n + 1)],
                                start=(k == 0), stop=(k == KC - 1))
                    r_mm[t] = smm.inc(ins)
                blk.tensor(rp)

            def rev_all(scalar, H=H):
                for t in range(8):
                    scalar.wait_ge(smm.h, r_mm[t])
                    v = sac.inc(scalar.activation(
                        SCRB[:, 1024 * t:1024 * (t + 1)], pr[t % 2][:],
                        AF.Sigmoid))
                    assert v == r_ev[t]
            blk.scalar(rev_all)
            r_d = sac.n
            es_r.close()

            es_k = ExitStack()
            pk2 = [es_k.enter_context(nc.psum_tensor(f"pk2{H}{i}", [128, 1024], F32))
                   for i in range(2)]
            krel2 = [SCRF[:, 1024:2048], self.SCRF2[:, 0:1024]]
            k_mm = [0] * KC
            krel_d = [0] * KC
            sacb_k = sac.n
            k_ev = [sacb_k + dk + 1 for dk in range(KC)]
            for dk in range(KC):
                def kp(tensor, dk=dk, H=H):
                    if dk == 0:
                        tensor.wait_ge(sdv.h, mixes_d)
                        tensor.wait_ge(sg.h, wk_ld[H])
                        tensor.wait_ge(sac.h, r_ev[7])
                    if dk >= 2:
                        tensor.wait_ge(sac.h, k_ev[dk - 2])
                    for n in range(2):
                        for k in range(KC):
                            ins = tensor.matmul(
                                pk2[dk % 2][:, 512 * n:512 * (n + 1)],
                                WS2[:, 1024 * k + 128 * dk:1024 * k + 128 * (dk + 1)],
                                xk[:, 1024 * k + 512 * n:1024 * k + 512 * (n + 1)],
                                start=(k == 0), stop=(k == KC - 1))
                    k_mm[dk] = smm.inc(ins)
                blk.tensor(kp)

                def kev(vector, dk=dk):
                    vector.wait_ge(smm.h, k_mm[dk])
                    if dk >= 2:
                        vector.wait_ge(sac.h, k_ev[dk - 2])
                    krel_d[dk] = sdv.inc(vector.tensor_scalar_max(
                        krel2[dk % 2], pk2[dk % 2][:], 0.0))
                blk.vector(kev)

            def kev2_all(scalar, H=H):
                for dk in range(KC):
                    scalar.wait_ge(sdv.h, krel_d[dk])
                    v = sac.inc(scalar.activation(
                        FMA[:, 8192 + 1024 * dk:8192 + 1024 * (dk + 1)],
                        krel2[dk % 2], AF.Square))
                    assert v == k_ev[dk]
            blk.scalar(kev2_all)
            kact_d = sac.n
            es_k.close()

            es_v = ExitStack()
            pv2 = [es_v.enter_context(nc.psum_tensor(f"pv2{H}{i}", [128, 1024], F32))
                   for i in range(2)]
            vsc = SCRF[:, 0:1024]
            v_mm = [0] * 8
            v_ev = [0] * 8
            for t in range(8):
                def vp2(tensor, t=t, H=H):
                    if t == 0:
                        tensor.wait_ge(sac.h, kact_d)
                        tensor.wait_ge(sg.h, wv_ld[H])
                    if t >= 2:
                        tensor.wait_ge(sdv.h, v_ev[t - 2])
                    for n in range(2):
                        for k in range(KC):
                            ins = tensor.matmul(
                                pv2[t % 2][:, 512 * n:512 * (n + 1)],
                                FMA[:, 8192 + 1024 * k + 128 * t:
                                    8192 + 1024 * k + 128 * (t + 1)],
                                FMA[:, 1024 * k + 512 * n:1024 * k + 512 * (n + 1)],
                                start=(k == 0), stop=(k == KC - 1))
                    v_mm[t] = smm.inc(ins)
                blk.tensor(vp2)

                def vev(vector, t=t, H=H):
                    vector.wait_ge(smm.h, v_mm[t])
                    if t >= 1:
                        vector.wait_ge(sdv.h, v_ev[t - 1])
                    vector.wait_ge(sac.h, r_d)
                    sdv.inc(vector.tensor_mul(vsc, pv2[t % 2][:],
                                              SCRB[:, 1024 * t:1024 * (t + 1)]))
                    sdv.wait(vector)
                    tg = 8 * H + t
                    xs = self.XM[:, 1024 * tg:1024 * (tg + 1)]
                    v_ev[t] = sdv.inc(vector.tensor_add(xs, xs, vsc))
                blk.vector(vev)
            es_v.close()
            h_done[H] = sdv.n

        self.x_l1f_done = sdv.n
        self.dbg("x_l1f", self.XM[:], self.sdv)


QKPERM = np.zeros(D, np.int64)
for _dg in range(8):
    for _a in range(16):
        for _di in range(8):
            QKPERM[128 * _dg + 8 * _a + _di] = 64 * _a + 8 * _dg + _di


def make_host_inputs(inp):
    x = np.asarray(inp["L0_attn_x"], np.float32).reshape(S, D)
    c = np.ascontiguousarray
    ident = np.eye(128, dtype=np.float32)
    mask_dd = np.zeros((128, 128), np.float32)
    for m in range(128):
        mask_dd[m, m % 8::8] = 1.0
    e16 = np.zeros((128, 16), np.float32)
    for a in range(16):
        e16[8 * a:8 * a + 8, a] = 1.0
    m128 = np.zeros((128, 128), np.float32)
    for b2 in range(2):
        for A in range(2):
            for dd in range(64):
                m128[64 * b2 + dd, 64 * A + dd] = 1.0
    s_allm = np.zeros((16, KC * 128), np.float32)
    for p in range(KC):
        for beta in range(2):
            s_allm[2 * p + beta, 128 * p + 64 * beta:128 * p + 64 * beta + 64] = 1.0
    ones_row = np.ones((1, 128), np.float32)

    import ml_dtypes
    bf = lambda a: np.asarray(a, np.float32).astype(ml_dtypes.bfloat16)

    maps = []
    for core in range(NCORES):
        hsm = np.zeros((16, 2), np.float32)
        hsm[2 * core, 0] = 1.0
        hsm[2 * core + 1, 1] = 1.0
        hsel = np.zeros((8, 1), np.float32)
        if core > 0:
            hsel[core - 1, 0] = 1.0
        m = {
            "x_slice": c(x[T * core:T * (core + 1)]),
            "L0_Wq": c(np.asarray(inp["L0_attn_Wq"], np.float32)[:, QKPERM]),
            "L0_Wk": c(np.asarray(inp["L0_attn_Wk"], np.float32)[:, QKPERM]),
            "L0_WvT": c(np.asarray(inp["L0_attn_Wv"], np.float32).T),
            "L0_Wo": c(np.asarray(inp["L0_attn_Wo"], np.float32)),
            "L0_bo_row": c(np.asarray(inp["L0_attn_bo"], np.float32).reshape(1, D)),
            "F0_w_slab": c(np.asarray(inp["L0_ffn_weight"], np.float32)[128 * core:128 * (core + 1)]),
            "F0_bias_row": c(np.asarray(inp["L0_ffn_bias"], np.float32).reshape(1, D)),
            "L1_WqT": c(np.asarray(inp["L1_attn_Wq"], np.float32).T),
            "L1_WkT": c(np.asarray(inp["L1_attn_Wk"], np.float32).T),
            "L1_WvT": c(np.asarray(inp["L1_attn_Wv"], np.float32).T),
            "L1_WoT": c(np.asarray(inp["L1_attn_Wo"], np.float32).T),
            "L1_bq_pp": c(np.asarray(inp["L1_attn_bq"], np.float32).reshape(128, 1)),
            "L1_bk_pp": c(np.asarray(inp["L1_attn_bk"], np.float32).reshape(128, 1)),
            "L1_bv_row": c(np.asarray(inp["L1_attn_bv"], np.float32).reshape(1, 128)),
            "L1_bo_row": c(np.asarray(inp["L1_attn_bo"], np.float32).reshape(1, D)),
            "L1_wav_pp": c(np.asarray(inp["L1_attn_wavelengths"], np.float32).reshape(128, 1)),
            "L1_phb_pp": c(np.asarray(inp["L1_attn_phase_bias"], np.float32).reshape(128, 1)),
            "F1_WrT": c(np.asarray(inp["L1_ffn_W_r"], np.float32).T),
            "F1_WkT": c(np.asarray(inp["L1_ffn_W_k"], np.float32).T),
            "F1_WvT": c(np.asarray(inp["L1_ffn_W_v"], np.float32).T),
            "F1_mur_pp": c(np.asarray(inp["L1_ffn_mu_r"], np.float32).reshape(KC, 128).T),
            "F1_muk_pp": c(np.asarray(inp["L1_ffn_mu_k"], np.float32).reshape(KC, 128).T),
            "identb": bf(ident),
            "mask_dd": bf(mask_dd),
            "e16sel": bf(e16),
            "mask128": bf(m128),
            "s_all": bf(s_allm),
            "hs": bf(hsm),
            "ones_col": ones_row,
            "ones_col_b": bf(ones_row),
            "halo_sel": bf(hsel),
        }
        maps.append(m)
    return maps


_MK_CACHE = {}


def _get_mega():
    if "mk" not in _MK_CACHE:
        _MK_CACHE["mk"] = Mega(upto=5, debug=())
    return _MK_CACHE["mk"]


def kernel(**inputs):
    mk = _get_mega()
    maps = make_host_inputs(inputs)
    res = run_bass_kernel_spmd(mk.nc, maps, core_ids=list(range(NCORES)))
    out = np.concatenate([np.asarray(res.results[c]["y"], np.float32)
                          for c in range(NCORES)], axis=0)
    return out.reshape(1, S, D)


# version 14 (final)
# speedup vs baseline: 1.0063x; 1.0063x over previous
import numpy as np
import concourse.bass as bass
import concourse.mybir as mybir
from concourse.bass_utils import run_bass_kernel_spmd
from contextlib import ExitStack

F32 = mybir.dt.float32
BF16 = mybir.dt.bfloat16
I32 = mybir.dt.int32
AF = mybir.ActivationFunctionType
ALU = mybir.AluOpType
AX = mybir.AxisListType

NCORES = 8
T = 2048
TC = 16
D = 1024
KC = 8
S = 16384
PI = float(np.pi)
INV2PI = float(1.0 / (2 * np.pi))
TWO_PI = float(2 * np.pi)
RS128 = float(1.0 / np.sqrt(128.0))
MAGIC = 12582912.0
CEXP = float(np.sqrt(128.0))

f32 = np.float32


class Sem:
    def __init__(self, es, nc, name):
        self.h = es.enter_context(nc.semaphore(name))
        self.n = 0

    def inc(self, instr, k=1):
        instr.then_inc(self.h, k)
        self.n += k
        return self.n

    def wait(self, eng, n=None):
        eng.wait_ge(self.h, self.n if n is None else n)


class Mega:
    def __init__(self, upto=99, debug=()):
        self.upto = upto
        self.debug = set(debug)
        self.dbg_pending = []
        nc = bass.Bass("TRN2", target_bir_lowering=False)
        self.nc = nc
        self.es = ExitStack()
        self.build()

    def din(self, name, shape, dtype=F32):
        return self.nc.dram_tensor(name, list(shape), dtype, kind="ExternalInput").ap()

    def dout(self, name, shape, dtype=F32):
        return self.nc.dram_tensor(name, list(shape), dtype, kind="ExternalOutput").ap()

    def dram(self, name, shape, dtype=F32, shared=False):
        return self.nc.dram_tensor(name, list(shape), dtype, kind="Internal",
                                   addr_space=("Shared" if shared else "Local")).ap()

    def sb(self, name, shape, dtype=F32):
        return self.es.enter_context(self.nc.sbuf_tensor(name, list(shape), dtype))

    def dbg(self, name, src_ap, sem):
        if name not in self.debug:
            return
        val = sem.n

        def emit(sync):
            out = self.dout("dbg_" + name, list(src_ap.shape), src_ap.dtype)
            sync.wait_ge(sem.h, val)
            self.sst.inc(sync.dma_start(out=out[:], in_=src_ap), 16)
        self.blk.sync(emit)

    def flush_dbg(self):
        pass

    def build(self):
        nc, es = self.nc, self.es
        U = self.upto

        x_in = self.din("x_slice", [T, D])
        self.wi = dict(
            wq0=self.din("L0_Wq", [D, D]), wk0=self.din("L0_Wk", [D, D]),
            wv0T=self.din("L0_WvT", [D, D]), wo0=self.din("L0_Wo", [D, D]),
            wfslab=self.din("F0_w_slab", [128, D]),
            wq1T=self.din("L1_WqT", [D, 128]), wk1T=self.din("L1_WkT", [D, 128]),
            wv1T=self.din("L1_WvT", [D, 128]), wo1T=self.din("L1_WoT", [128, D]),
            wr1T=self.din("F1_WrT", [D, D]), wkf1T=self.din("F1_WkT", [D, D]),
            wvf1T=self.din("F1_WvT", [D, D]),
        )
        bo0 = self.din("L0_bo_row", [1, D])
        bf0 = self.din("F0_bias_row", [1, D])
        bq1 = self.din("L1_bq_pp", [128, 1]); bk1 = self.din("L1_bk_pp", [128, 1])
        bv1r = self.din("L1_bv_row", [1, 128]); bo1r = self.din("L1_bo_row", [1, D])
        wav = self.din("L1_wav_pp", [128, 1]); phb = self.din("L1_phb_pp", [128, 1])
        mur = self.din("F1_mur_pp", [128, KC]); muk = self.din("F1_muk_pp", [128, KC])
        identb = self.din("identb", [128, 128], BF16)
        mask_dd = self.din("mask_dd", [128, 128], BF16)
        e16sel = self.din("e16sel", [128, 16], BF16)
        mask128 = self.din("mask128", [128, 128], BF16)
        s_all = self.din("s_all", [16, KC * 128], BF16)
        hs = self.din("hs", [16, 2], BF16)
        ones_col = self.din("ones_col", [1, 128])
        ones_col_b = self.din("ones_col_b", [1, 128], BF16)
        halo_sel = self.din("halo_sel", [8, 1], BF16)

        self.y_out = self.dout("y", [T, D])

        self.d_ln0_in = self.dram("b_ln0_in", [KC * 128, T], BF16)
        self.d_ln0_out = self.dram("b_ln0_out", [NCORES * KC * 128, T], BF16, shared=True)
        self.d_en_in = self.dram("b_en_in", [16, 16])
        self.d_en_out = self.dram("b_en_out", [16, 16])
        self.d_wf_in = self.dram("b_wf_in", [KC, 128 * 128], BF16)
        self.d_wf_out = self.dram("b_wf_out", [NCORES * KC, 128 * 128], BF16, shared=True)
        self.d_kv_in = self.dram("b_kv_in", [3 * 128, T], BF16)
        self.d_kv_out = self.dram("b_kv_out", [NCORES * 3 * 128, T], BF16, shared=True)
        self.d_halo_in = self.dram("b_halo_in", [1, D], BF16)
        self.d_halo_out = self.dram("b_halo_out", [NCORES, D], BF16, shared=True)

        self.XM = self.sb("XM", [128, TC * 1024], F32)
        self.FMA = self.sb("FMA", [128, 16384], BF16)
        self.UTM = self.sb("UTM", [128, 16384], BF16)
        self.SCRF = self.sb("SCRF", [128, 3072], F32)
        self.SCRB = self.sb("SCRB", [128, 8192], BF16)
        self.WS1 = self.sb("WS1", [128, 8192], BF16)
        self.WS2 = self.sb("WS2", [128, 8192], BF16)
        self.SM = self.sb("SM", [128, 384], F32)
        self.SCRF2 = self.sb("SCRF2", [128, 1024], F32)
        CB = self.sb("CB", [128, 400], BF16)
        self.SALL = self.sb("SALL", [16, KC * 128], BF16)
        self.HS = self.sb("HS", [16, 2], BF16)
        self.HSEL = self.sb("HSEL", [8, 1], BF16)
        self.PP = self.sb("PP", [128, 32], F32)
        self.ONEC = self.sb("ONEC", [1, 128], F32)
        self.ONEB = self.sb("ONEBt", [1, 128], BF16)
        self.BROWS = self.sb("BROWS", [1, 5376], BF16)
        self.MASK_DD = CB[:, 0:128]
        self.MASK128 = CB[:, 128:256]
        self.E16 = CB[:, 256:272]
        self.IDB = CB[:, 272:400]

        self.sq = Sem(es, nc, "s_q")
        self.sg = Sem(es, nc, "s_g")
        self.smm = Sem(es, nc, "s_mm")
        self.sdv = Sem(es, nc, "s_dv")
        self.sac = Sem(es, nc, "s_ac")
        self.scc = Sem(es, nc, "s_cc")
        self.scc1 = Sem(es, nc, "s_cc1")
        self.sqP = [Sem(es, nc, "s_qp0"), Sem(es, nc, "s_qp1")]
        self.scc2 = Sem(es, nc, "s_cc2")
        self.scc3 = Sem(es, nc, "s_cc3")
        self.scc4 = Sem(es, nc, "s_cc4")
        self.sst = Sem(es, nc, "s_st")
        sq = self.sq

        blk = es.enter_context(nc.Block())
        self.blk = blk

        self.sxq = [Sem(es, nc, f"s_x{i}") for i in range(4)]
        xm_t = self.XM.ap().rearrange("p (t d) -> p t d", d=D)
        xin_t = x_in.rearrange("(t p) d -> p t d", p=128)

        def init_loads(sync):
            q = lambda out, in_: sq.inc(sync.dma_start(out=out, in_=in_), 16)
            self.sxq[0].inc(sync.dma_start(
                out=xm_t[:, 0:4], in_=xin_t[:, 0:4]), 16)
            self.sxq[3].inc(sync.dma_start(
                out=xm_t[:, 12:16], in_=xin_t[:, 12:16]), 16)
            q(CB[:, 0:128], mask_dd[:])
            q(CB[:, 128:256], mask128[:])
            q(CB[:, 256:272], e16sel[:])
            q(CB[:, 272:400], identb[:])
            q(self.SALL[:], s_all[:])
            q(self.HS[:], hs[:])
            q(self.HSEL[:], halo_sel[:])
            q(self.ONEC[:], ones_col[:])
            q(self.ONEB[:], ones_col_b[:])
            q(self.PP[:, 0:KC], mur[:])
            q(self.PP[:, 8:16], muk[:])
            q(self.PP[:, 16:17], wav[:])
            q(self.PP[:, 17:18], phb[:])
            q(self.PP[:, 18:19], bq1[:])
            q(self.PP[:, 19:20], bk1[:])
        blk.sync(init_loads)
        self.loads_done = sq.n

        def xload_act(scalar):
            self.sxq[1].inc(scalar.dma_start(
                out=xm_t[:, 4:8], in_=xin_t[:, 4:8]), 16)
            self.sxq[2].inc(scalar.dma_start(
                out=xm_t[:, 8:12], in_=xin_t[:, 8:12]), 16)
        blk.scalar(xload_act)

        def brow_loads(gpsimd):
            g = lambda out, in_: self.sg.inc(gpsimd.dma_start(out=out, in_=in_), 16)
            g(self.BROWS[0:1, 0:D], bo0[:])
            g(self.BROWS[0:1, D:2 * D], bf0[:])
            g(self.BROWS[0:1, 2 * D:3 * D], bo1r[:])
            g(self.BROWS[0:1, 3 * D:3 * D + 128], bv1r[:])
        blk.gpsimd(brow_loads)
        self.brows_done_g = self.sg.n

        self.phase_ln0()
        if U >= 2:
            self.phase_l0_attn()
        if U >= 3:
            self.phase_l0_ffn()
        if U >= 4:
            self.phase_l1_attn()
        if U >= 5:
            self.phase_l1_ffn()

        self.flush_dbg()

        def fin(sync):
            self.sdv.wait(sync)
            self.sac.wait(sync)
            self.smm.wait(sync)
            self.sst.inc(sync.dma_start(
                out=self.y_out.rearrange("(t p) d -> p t d", p=128),
                in_=self.XM.ap().rearrange("p (t d) -> p t d", d=D)), 16)
            self.sst.wait(sync)
        blk.sync(fin)

    def ln_stats(self, stats, extra_minmax=False, wait=None, chunk_wait=None):
        blk = self.blk
        XM = self.XM
        bn = self.SCRF[:, 0:192]
        CM, CV, CI, CN, CX, CNM, CSC = 0, 16, 32, 48, 64, 80, 96
        sdv, sac = self.sdv, self.sac

        def stats_dve(vector):
            if wait is not None:
                wait(vector)
            for t in range(TC):
                if chunk_wait is not None:
                    chunk_wait(vector, t)
                xs = XM[:, 1024 * t:1024 * (t + 1)]
                vector.bn_stats(bn[:, 12 * t:12 * t + 6], xs[:, 0:512])
                ins = vector.bn_stats(bn[:, 12 * t + 6:12 * t + 12], xs[:, 512:1024])
                if extra_minmax:
                    vector.tensor_reduce(stats[:, CX + t:CX + t + 1], xs, AX.X, ALU.max)
                    ins = vector.tensor_reduce(stats[:, CNM + t:CNM + t + 1], xs,
                                               AX.X, ALU.min, negate=True)
            sdv.inc(ins)
            sdv.wait(vector)
            for t in range(TC):
                ins = vector.bn_aggr(stats[:, CSC + 2 * (t % 8):CSC + 2 * (t % 8) + 2],
                                     bn[:, 12 * t:12 * t + 12])
                sdv.inc(ins)
                sdv.wait(vector)
                ins = vector.tensor_copy(stats[:, CM + t:CM + t + 1],
                                         stats[:, CSC + 2 * (t % 8):CSC + 2 * (t % 8) + 1])
                ins = vector.tensor_scalar(
                    stats[:, CV + t:CV + t + 1],
                    stats[:, CSC + 2 * (t % 8) + 1:CSC + 2 * (t % 8) + 2],
                    1.0, 1e-5, ALU.mult, ALU.add)
            sdv.inc(ins)
        blk.vector(stats_dve)

        def stats_sqrt(scalar):
            sdv.wait(scalar)
            self.sac.inc(scalar.activation(
                stats[:, CI:CI + 16], stats[:, CV:CV + 16], AF.Sqrt))
        blk.scalar(stats_sqrt)

        def stats_fin(vector):
            sac.wait(vector)
            std = stats[:, CI:CI + 16]
            inv0 = stats[:, CSC:CSC + 16]
            sdv.inc(vector.reciprocal(inv0, std))
            sdv.wait(vector)
            a = stats[:, CSC + 16:CSC + 32]
            sdv.inc(vector.tensor_mul(a, inv0, inv0))
            sdv.wait(vector)
            sdv.inc(vector.tensor_mul(a, a, stats[:, CV:CV + 16]))
            sdv.wait(vector)
            sdv.inc(vector.tensor_scalar(a, a, -0.5, 1.5, ALU.mult, ALU.add))
            sdv.wait(vector)
            sdv.inc(vector.tensor_mul(stats[:, CI:CI + 16], inv0, a))
            sdv.wait(vector)
            sdv.inc(vector.scalar_tensor_tensor(
                stats[:, CN:CN + 16], stats[:, CM:CM + 16], -1.0,
                stats[:, CI:CI + 16], ALU.mult, ALU.mult))
        blk.vector(stats_fin)
        return dict(inv=CI, negmi=CN, m=CM, xmax=CX, negxmin=CNM, v=CV)

    def ln_apply(self, stats, cols, dst_tm, n0=0, n1=TC):
        blk = self.blk
        base = self.sac.n

        def ap(scalar):
            self.sdv.wait(scalar)
            for t in range(n0, n1):
                self.sac.inc(scalar.activation(
                    dst_tm[:, 1024 * (t - n0):1024 * (t - n0 + 1)],
                    self.XM[:, 1024 * t:1024 * (t + 1)],
                    AF.Identity,
                    bias=stats[:, cols["negmi"] + t:cols["negmi"] + t + 1],
                    scale=stats[:, cols["inv"] + t:cols["inv"] + t + 1]))
        blk.scalar(ap)
        return base

    def transpose_tm_fm(self, tag, src_tm, fm, n_tc=TC, wait_sem=None,
                        wait_val=None, src_cols=1024, order="k",
                        group_wait=None, t_off=0, fm_kstride=None,
                        pre_wait=None):
        nc, blk = self.nc, self.blk
        es = ExitStack()
        pb = [es.enter_context(nc.psum_tensor(f"{tag}_tp{i}", [128, 512], BF16))
              for i in range(2)]
        smm, sdv = self.smm, self.sdv
        kc = src_cols // 128
        if order == "t":
            groups = [(k, t0) for t0 in range(0, n_tc, 4) for k in range(kc)]
        else:
            groups = [(k, t0) for k in range(kc) for t0 in range(0, n_tc, 4)]
        mm_base = smm.n
        dv_base = sdv.n

        def tp_mm(tensor):
            if pre_wait is not None:
                pre_wait(tensor)
            if wait_sem is not None and group_wait is None:
                tensor.wait_ge(wait_sem.h, wait_val)
            last_gw = -1
            for gi, (k, t0) in enumerate(groups):
                if group_wait is not None:
                    gv = group_wait(t0)
                    if gv > last_gw:
                        tensor.wait_ge(wait_sem.h, gv)
                        last_gw = gv
                b = pb[gi % 2]
                if gi >= 2:
                    tensor.wait_ge(sdv.h, dv_base + gi - 1)
                for j in range(4):
                    t = t0 + j
                    ins = tensor.transpose(
                        b[:, 128 * j:128 * (j + 1)],
                        src_tm[:, src_cols * t + 128 * k:src_cols * t + 128 * (k + 1)],
                        self.IDB[:])
                smm.inc(ins)
        blk.tensor(tp_mm)

        ks = fm_kstride if fm_kstride is not None else n_tc * 128

        def tp_dv(vector):
            for gi, (k, t0) in enumerate(groups):
                vector.wait_ge(smm.h, mm_base + gi + 1)
                dst = fm[:, k * ks + 128 * (t_off + t0):
                         k * ks + 128 * (t_off + t0 + 4)]
                self.sdv.inc(vector.tensor_copy(dst, pb[gi % 2][:]))
        blk.vector(tp_dv)
        es.close()
        return dv_base

    def lw_inline(self, gpsimd, hbm_ap, dst, rows, cols):
        kc = rows // 128
        self.sg.inc(gpsimd.dma_start(
            out=dst.rearrange("p (k c) -> p k c", c=cols)[:, 0:kc, :],
            in_=hbm_ap.rearrange("(k p) c -> p k c", p=128)), 16)

    def load_weight_bf16(self, hbm_ap, dst, rows, cols):
        kc = rows // 128

        def ld(gpsimd):
            self.sg.inc(gpsimd.dma_start(
                out=dst.rearrange("p (k c) -> p k c", c=cols)[:, 0:kc, :],
                in_=hbm_ap.rearrange("(k p) c -> p k c", p=128)), 16)
        self.blk.gpsimd(ld)
        return self.sg.n

    def phase_ln0(self):
        stats = self.SM[:, 0:128]

        def cw(vector, t):
            if t % 4 == 0:
                vector.wait_ge(self.sxq[t // 4].h, 16)
        self.ln0_stats_cols = self.ln_stats(stats, chunk_wait=cw)
        self.ln0_ap_base = self.ln_apply(stats, self.ln0_stats_cols, self.UTM)
        self.dbg("ln0_fm", self.FMA[:], self.sdv)
        self.dbg("ln0_stats", self.SM[:, 0:128], self.sdv)
        self.dbg("ln0_aux", self.SM[:, 128:384], self.sdv)
        self.dbg("ln0_utm", self.UTM[:], self.sac)

    def phase_l0_attn(self):
        nc, blk = self.nc, self.blk
        sq, sg, smm, sdv, sac, scc = (self.sq, self.sg, self.smm, self.sdv,
                                      self.sac, self.scc)
        FMA, UTM, SCRB, SM = self.FMA, self.UTM, self.SCRB, self.SM
        WS1, WS2 = self.WS1, self.WS2
        BROWS, ONEB = self.BROWS, self.ONEB


        wq_ld = self.load_weight_bf16(self.wi["wq0"], WS1.ap(), D, D)
        wk_ld = self.load_weight_bf16(self.wi["wk0"], WS2.ap(), D, D)


        es_en = ExitStack()
        PE1 = es_en.enter_context(nc.psum_tensor("pe1", [128, 512], F32))
        PE2 = es_en.enter_context(nc.psum_tensor("pe2", [128, 512], F32))
        es_qk = ExitStack()
        pq = es_qk.enter_context(nc.psum_tensor("pq", [128, 1024], F32))
        pk = es_qk.enter_context(nc.psum_tensor("pk", [128, 1024], F32))

        def hilo(par, which):
            base = 4096 * par
            off = {"qhi": 0, "qlo": 1024, "khi": 2048, "klo": 3072}[which]
            return SCRB[:, base + off:base + off + 1024]

        q_mm = [0] * TC; k_mm = [0] * TC
        qhi_d = [0] * TC; khi_d = [0] * TC
        qlo_d = [0] * TC; klo_d = [0] * TC
        en_d = [0] * TC

        tpb = self.transpose_tm_fm(
            "ln0", UTM, FMA, n_tc=TC, wait_sem=sac, order="t",
            group_wait=lambda t0: self.ln0_ap_base + t0 + 4,
            pre_wait=lambda tensor: tensor.wait_ge(sq.h, self.loads_done))
        tp_done = [tpb + 8 * (g + 1) for g in range(4)]

        d_in_kt = self.d_ln0_in.rearrange("(k p) t -> p k t", p=128)
        fma_kt = self.FMA.ap().rearrange("p (k t) -> p k t", t=T)

        def bw(sync):
            sync.wait_ge(sq.h, sq.n)
            for g in range(4):
                sync.wait_ge(sdv.h, tp_done[g])
                sq.inc(sync.dma_start(
                    out=d_in_kt[:, :, 512 * g:512 * (g + 1)],
                    in_=fma_kt[:, :, 512 * g:512 * (g + 1)]), 16)
        blk.sync(bw)
        ln0_bounce_done = sq.n

        def ag_ln0(gpsimd):
            gpsimd.wait_ge(sq.h, ln0_bounce_done)
            self.scc1.inc(gpsimd.collective_compute(
                "AllGather", ALU.bypass, replica_groups=[list(range(NCORES))],
                ins=[self.d_ln0_in[:]], outs=[self.d_ln0_out[:]]))
        blk.gpsimd(ag_ln0)
        ag_ln0_done = self.scc1.n

        for t in range(TC):
            par = t % 2

            def pe_qk(tensor, t=t, par=par):
                if t == 0:
                    tensor.wait_ge(sg.h, wk_ld)
                if t % 4 == 0:
                    tensor.wait_ge(sdv.h, tp_done[t // 4])
                if t >= 1:
                    tensor.wait_ge(sdv.h, qlo_d[t - 1])
                for k in range(KC):
                    for n in range(2):
                        ins = tensor.matmul(
                            pq[:, 512 * n:512 * (n + 1)],
                            FMA[:, k * T + 128 * t:k * T + 128 * (t + 1)],
                            WS1[:, 1024 * k + 512 * n:1024 * k + 512 * (n + 1)],
                            start=(k == 0), stop=(k == KC - 1))
                q_mm[t] = smm.inc(ins)
                if t >= 1:
                    tensor.wait_ge(sdv.h, klo_d[t - 1])
                for k in range(KC):
                    for n in range(2):
                        ins = tensor.matmul(
                            pk[:, 512 * n:512 * (n + 1)],
                            FMA[:, k * T + 128 * t:k * T + 128 * (t + 1)],
                            WS2[:, 1024 * k + 512 * n:1024 * k + 512 * (n + 1)],
                            start=(k == 0), stop=(k == KC - 1))
                k_mm[t] = smm.inc(ins)
            blk.tensor(pe_qk)

            def dve_evac(vector, t=t, par=par):
                if t >= 2:
                    vector.wait_ge(smm.h, en_d[t - 2])
                vector.wait_ge(smm.h, q_mm[t])
                qhi_d[t] = sdv.inc(vector.tensor_copy(hilo(par, "qhi"), pq[:]))
                sdv.wait(vector)
                qlo_d[t] = sdv.inc(vector.scalar_tensor_tensor(
                    hilo(par, "qlo"), pq[:], 1.0, hilo(par, "qhi"),
                    ALU.mult, ALU.subtract))
                vector.wait_ge(smm.h, k_mm[t])
                khi_d[t] = sdv.inc(vector.tensor_copy(hilo(par, "khi"), pk[:]))
                sdv.wait(vector)
                klo_d[t] = sdv.inc(vector.scalar_tensor_tensor(
                    hilo(par, "klo"), pk[:], 1.0, hilo(par, "khi"),
                    ALU.mult, ALU.subtract))
            blk.vector(dve_evac)

            def pe_energy(tensor, t=t, par=par):
                tensor.wait_ge(sdv.h, qlo_d[t])
                tensor.wait_ge(sdv.h, klo_d[t])
                for dg in range(8):
                    slot = (PE1 if dg < 4 else PE2)[:, 128 * (dg % 4):128 * (dg % 4) + 128]
                    for pi, (lw, rw) in enumerate(
                            [("qhi", "khi"), ("qhi", "klo"), ("qlo", "khi")]):
                        ins = tensor.matmul(
                            slot, hilo(par, lw)[:, 128 * dg:128 * (dg + 1)],
                            hilo(par, rw)[:, 128 * dg:128 * (dg + 1)],
                            start=(t == 0 and pi == 0 and dg % 4 == 0),
                            stop=(t == TC - 1 and pi == 2 and dg % 4 == 3))
                en_d[t] = smm.inc(ins)
            blk.tensor(pe_energy)

        masked = lambda dg: SCRB[:, 128 * dg:128 * (dg + 1)]

        def dve_mask(vector):
            vector.wait_ge(smm.h, en_d[TC - 1])
            for dg in range(8):
                slot = (PE1 if dg < 4 else PE2)[:, 128 * (dg % 4):128 * (dg % 4) + 128]
                ins = vector.tensor_tensor(masked(dg), slot, self.MASK_DD, ALU.mult)
            sdv.inc(ins)
        blk.vector(dve_mask)
        mask_done = sdv.n
        self.dbg("msk0", SCRB[:, 0:128], self.sdv)
        self.dbg("msk7", SCRB[:, 896:1024], self.sdv)
        es_qk.close()

        es_epi = ExitStack()
        p2 = es_epi.enter_context(nc.psum_tensor("p2", [16, 128], F32))

        def pe_e16(tensor):
            tensor.wait_ge(sdv.h, mask_done)
            for dg in range(8):
                ins = tensor.matmul(p2[:], self.E16, masked(dg),
                                    start=(dg == 0), stop=(dg == 7))
            smm.inc(ins)
        blk.tensor(pe_e16)
        e16_done = smm.n

        EN_P = SM[0:16, 128:144]

        def dve_ered(vector):
            vector.wait_ge(smm.h, e16_done)
            s1 = SM[0:16, 208:272]
            s2 = SM[0:16, 272:304]
            s0 = SM[0:16, 240:368][:, 0:128]
            sdv.inc(vector.tensor_copy(s0, p2.ap()))
            sdv.wait(vector)
            sdv.inc(vector.tensor_add(s1, s0[:, 0:127:2], s0[:, 1:128:2]))
            sdv.wait(vector)
            sdv.inc(vector.tensor_add(s2, s1[:, 0:63:2], s1[:, 1:64:2]))
            sdv.wait(vector)
            sdv.inc(vector.tensor_add(EN_P, s2[:, 0:31:2], s2[:, 1:32:2]))
        blk.vector(dve_ered)
        ered_done = sdv.n

        def en_ar_w(sync):
            sync.wait_ge(sdv.h, ered_done)
            sync.wait_ge(sq.h, sq.n)
            sq.inc(sync.dma_start(out=self.d_en_in[:], in_=EN_P), 16)
        blk.sync(en_ar_w)
        en_bounce = sq.n

        def en_cc(gpsimd):
            gpsimd.wait_ge(sq.h, en_bounce)
            self.scc2.inc(gpsimd.collective_compute(
                "AllReduce", ALU.add, replica_groups=[list(range(NCORES))],
                ins=[self.d_en_in[:]], outs=[self.d_en_out[:]]))
        blk.gpsimd(en_cc)
        en_ar_done = self.scc2.n

        EN = SM[0:16, 144:160]

        def en_back(sync):
            sync.wait_ge(self.scc2.h, en_ar_done)
            sync.wait_ge(sq.h, sq.n)
            sq.inc(sync.dma_start(out=EN, in_=self.d_en_out[:]), 16)
        blk.sync(en_back)
        en_back_done = sq.n

        ATT = SM[0:16, 160:176]
        att_bf = SCRB[0:16, 4096:4112]

        def sm1(vector):
            vector.wait_ge(sq.h, en_back_done)
            nm = SM[0:16, 176:177]
            sdv.inc(vector.tensor_reduce(nm, EN, AX.X, ALU.max, negate=True))
            sdv.wait(vector)
            sdv.inc(vector.tensor_scalar_mul(SM[0:16, 177:178], nm, 0.125))
        blk.vector(sm1)

        def sm2(scalar):
            sdv.wait(scalar)
            sac.inc(scalar.activation(ATT, EN, AF.Exp,
                                      bias=SM[0:16, 177:178],
                                      scale=np.float32(0.125),
                                      accum_out=SM[0:16, 178:179]))
        blk.scalar(sm2)

        def sm3(vector):
            sac.wait(vector)
            sdv.inc(vector.reciprocal(SM[0:16, 179:180], SM[0:16, 178:179]))
            sdv.wait(vector)
            sdv.inc(vector.tensor_scalar_mul(ATT, ATT, SM[0:16, 179:180]))
            sdv.wait(vector)
            sdv.inc(vector.tensor_copy(att_bf, ATT))
        blk.vector(sm3)
        att_done = sdv.n
        self.dbg("energy", EN, self.sq)
        self.dbg("en_p", EN_P, self.sdv)
        self.dbg("att", ATT, self.sdv)

        def ld_wvt(gpsimd):
            gpsimd.wait_ge(smm.h, en_d[TC - 1])
            self.lw_inline(gpsimd, self.wi["wv0T"], WS1.ap(), D, D)
        blk.gpsimd(ld_wvt)
        wvt_ld = self.sg.n

        def ld_wo(gpsimd):
            self.lw_inline(gpsimd, self.wi["wo0"], WS2.ap(), D, D)
        blk.gpsimd(ld_wo)
        wo_ld = self.sg.n

        ph = es_epi.enter_context(nc.psum_tensor("ph", [16, 2], F32))
        px = es_epi.enter_context(nc.psum_tensor("px", [128, 16], F32))
        attm = SCRB[0:16, 4112:4114]
        Xall = SM[:, 192:208]

        def pe_attm(tensor):
            tensor.wait_ge(sdv.h, att_done)
            smm.inc(tensor.matmul(ph[:], att_bf, self.HS[:], start=True, stop=True))
        blk.tensor(pe_attm)
        attm_mm = smm.n

        def dv_attm(vector):
            vector.wait_ge(smm.h, attm_mm)
            sdv.inc(vector.tensor_copy(attm, ph[:]))
        blk.vector(dv_attm)
        attm_done = sdv.n

        def pe_x(tensor):
            tensor.wait_ge(sdv.h, attm_done)
            for p in range(KC):
                ins = tensor.matmul(px[:, 2 * p:2 * p + 2],
                                    self.SALL[:, 128 * p:128 * (p + 1)], attm,
                                    start=True, stop=True)
            smm.inc(ins)
        blk.tensor(pe_x)
        x_mm = smm.n

        def dv_x(vector):
            vector.wait_ge(smm.h, x_mm)
            sdv.inc(vector.tensor_copy(Xall, px[:]))
        blk.vector(dv_x)
        xall_done = sdv.n

        E_sb = SCRB[:, 0:1024]
        fold_sb = SCRB[:, 1024:2048]

        def dv_ebuild(vector):
            vector.wait_ge(sdv.h, xall_done)
            for p in range(KC):
                for A in range(2):
                    ins = vector.tensor_scalar_mul(
                        E_sb[:, 128 * p + 64 * A:128 * p + 64 * A + 64],
                        self.MASK128[:, 64 * A:64 * A + 64],
                        Xall[:, 2 * p + A:2 * p + A + 1])
            sdv.inc(ins)
        blk.vector(dv_ebuild)
        ebuild_done = sdv.n

        pf = es_epi.enter_context(nc.psum_tensor("pf", [128, 128], F32))
        fold_mm = [0] * KC
        fold_ev = [0] * KC
        for i in range(KC):
            def pe_fold_i(tensor, i=i):
                if i == 0:
                    tensor.wait_ge(sdv.h, ebuild_done)
                    tensor.wait_ge(sg.h, wo_ld)
                else:
                    tensor.wait_ge(sdv.h, fold_ev[i - 1])
                for p in range(KC):
                    ins = tensor.matmul(
                        pf[:], WS1[:, 1024 * p + 128 * i:1024 * p + 128 * (i + 1)],
                        E_sb[:, 128 * p:128 * (p + 1)],
                        start=(p == 0), stop=(p == KC - 1))
                fold_mm[i] = smm.inc(ins)
            blk.tensor(pe_fold_i)

            def dv_fold_i(vector, i=i):
                vector.wait_ge(smm.h, fold_mm[i])
                fold_ev[i] = sdv.inc(vector.tensor_copy(
                    fold_sb[:, 128 * i:128 * (i + 1)], pf[:]))
            blk.vector(dv_fold_i)
        fold_done = sdv.n
        es_epi.close()

        SCRF = self.SCRF
        wsl = SCRF[:, 1024:2048]

        def slab_ld(sync):
            sync.wait_ge(sq.h, sq.n)
            sq.inc(sync.dma_start(out=wsl, in_=self.wi["wfslab"][:]), 16)
        blk.sync(slab_ld)
        slab_ld_d = sq.n

        w_eff = SCRB[:, 2048:3072]

        def slab_quant(vector):
            vector.wait_ge(sq.h, slab_ld_d)
            vector.wait_ge(smm.h, en_d[TC - 1])
            aw = SM[:, 192:193]
            sdv.inc(vector.tensor_reduce(aw, wsl, AX.X, ALU.max,
                                         apply_absolute_value=True))
            sdv.wait(vector)
            sdv.inc(vector.tensor_scalar_max(aw, aw, 1e-8))
            sdv.wait(vector)
            rw = SM[:, 193:194]
            sdv.inc(vector.reciprocal(rw, aw))
            sdv.wait(vector)
            sdv.inc(vector.tensor_scalar_mul(rw, rw, 127.0))
            sdv.wait(vector)
            sw = SM[:, 194:195]
            sdv.inc(vector.tensor_scalar_mul(sw, aw, 1.0 / 127.0))
            sdv.wait(vector)
            sdv.inc(vector.tensor_scalar_mul(wsl, wsl, rw))
            sdv.wait(vector)
            sdv.inc(vector.tensor_scalar(wsl, wsl, MAGIC, MAGIC,
                                         ALU.add, ALU.subtract))
            sdv.wait(vector)
            sdv.inc(vector.tensor_scalar_mul(w_eff, wsl, sw))
        blk.vector(slab_quant)
        slab_q_d = sdv.n

        wT = SCRB[:, 3072:4096]
        es_sl = ExitStack()
        psl = es_sl.enter_context(nc.psum_tensor("pslab", [128, 512], BF16))
        sl_mm = [0, 0]
        sl_ev = [0, 0]
        for g in range(2):
            def sl_tp(tensor, g=g):
                if g == 0:
                    tensor.wait_ge(sdv.h, slab_q_d)
                else:
                    tensor.wait_ge(sdv.h, sl_ev[g - 1])
                for j in range(4):
                    k = 4 * g + j
                    ins = tensor.transpose(psl[:, 128 * j:128 * (j + 1)],
                                           w_eff[:, 128 * k:128 * (k + 1)],
                                           self.IDB[:])
                sl_mm[g] = smm.inc(ins)
            blk.tensor(sl_tp)

            def sl_ev_f(vector, g=g):
                vector.wait_ge(smm.h, sl_mm[g])
                sl_ev[g] = sdv.inc(vector.tensor_copy(
                    wT[:, 512 * g:512 * (g + 1)], psl[:]))
            blk.vector(sl_ev_f)
        es_sl.close()
        slab_t_d = sdv.n

        def wf_bounce(sync):
            sync.wait_ge(sdv.h, slab_t_d)
            sync.wait_ge(sq.h, sq.n)
            sq.inc(sync.dma_start(
                out=self.d_wf_in.rearrange("k (p d) -> p k d", p=128),
                in_=wT.rearrange("p (k d) -> p k d", d=128)), 16)
        blk.sync(wf_bounce)
        wf_b_d = sq.n

        def wf_cc(gpsimd):
            gpsimd.wait_ge(sq.h, wf_b_d)
            self.scc3.inc(gpsimd.collective_compute(
                "AllGather", ALU.bypass, replica_groups=[list(range(NCORES))],
                ins=[self.d_wf_in[:]], outs=[self.d_wf_out[:]]))
        blk.gpsimd(wf_cc)
        wf_cc_d = self.scc3.n

        def wf_back(gpsimd):
            gpsimd.wait_ge(self.scc3.h, wf_cc_d)
            gpsimd.wait_ge(sdv.h, fold_done)
            gpsimd.wait_ge(sg.h, sg.n)
            srcall = self.d_wf_out.rearrange("(r k) (p d) -> k p r d", r=NCORES, p=128)
            for k in range(KC):
                dst = WS1[:, 1024 * k:1024 * (k + 1)].rearrange(
                    "p (r d) -> p r d", d=128)
                sg.inc(gpsimd.dma_start(out=dst, in_=srcall[k]), 16)
        blk.gpsimd(wf_back)
        self.wf_back_d = sg.n



        es_rt = ExitStack()
        pvm = [es_rt.enter_context(nc.psum_tensor(f"pvm{i}", [128, 512], F32))
               for i in range(2)]
        pz = [es_rt.enter_context(nc.psum_tensor(f"pz{i}", [128, 512], F32))
              for i in range(2)]

        SB_STRM = [0, 4096]
        SB_WIN = [8192, 10240]
        SB_ZW = [[12288, 13312], [14336, 15360]]

        strm_d = [0] * 32
        vm_mm = [0] * 32
        vm_ev = [0] * 32
        dup_d = [0] * 8
        z_mm = {}
        z_ev = {}
        ln0out_r = self.d_ln0_out.rearrange("(r k p) t -> p r k t", p=128, k=KC)

        def emit_strm(n):
            par = n % 2

            def strm(eng, n=n, par=par):
                if n == 0:
                    eng.wait_ge(self.scc1.h, ag_ln0_done)
                if n == 1:
                    eng.wait_ge(sdv.h, fold_done)
                if n >= 2:
                    eng.wait_ge(smm.h, vm_mm[n - 2])
                R, c0 = n // 4, (n % 4) * 512
                dst = UTM[:, SB_STRM[par]:SB_STRM[par] + 4096].rearrange(
                    "p (k c) -> p k c", c=512)
                strm_d[n] = self.sqP[par].inc(eng.dma_start(
                    out=dst, in_=ln0out_r[:, R, :, c0:c0 + 512]), 16)
            (blk.sync if par == 0 else blk.scalar)(strm)

        def emit_vm(n):
            par = n % 2
            rt, nj = n // 4, n % 4
            wpar = rt % 2

            def pe_vm(tensor, n=n, par=par):
                if n == 0:
                    tensor.wait_ge(sdv.h, fold_done)
                tensor.wait_ge(self.sqP[par].h, strm_d[n])
                if n >= 2:
                    tensor.wait_ge(sdv.h, vm_ev[n - 2])
                buf = UTM[:, SB_STRM[par]:SB_STRM[par] + 4096]
                for i in range(KC):
                    ins = tensor.matmul(pvm[par][:],
                                        fold_sb[:, 128 * i:128 * (i + 1)],
                                        buf[:, 512 * i:512 * (i + 1)],
                                        start=(i == 0), stop=(i == KC - 1))
                vm_mm[n] = smm.inc(ins)
            blk.tensor(pe_vm)

            def dv_vm(vector, n=n, par=par, rt=rt, nj=nj, wpar=wpar):
                vector.wait_ge(smm.h, vm_mm[n])
                if nj == 0 and rt >= 2:
                    vector.wait_ge(sg.h, dup_d[rt - 2])
                src = pvm[par].ap().rearrange("p (r2 j) -> p j r2", j=16)
                win = UTM[:, SB_WIN[wpar]:SB_WIN[wpar] + 2048]
                dst2 = win.rearrange("p (j r) -> p j r", j=16)[:, :, 32 * nj:32 * nj + 32]
                vm_ev[n] = sdv.inc(vector.tensor_copy(dst2, src))
            blk.vector(dv_vm)

        def emit_dup(rt):
            wpar = rt % 2

            def dup(gpsimd, rt=rt, wpar=wpar):
                gpsimd.wait_ge(sdv.h, vm_ev[4 * rt + 3])
                if rt == 0:
                    gpsimd.wait_ge(sg.h, self.wf_back_d)
                else:
                    gpsimd.wait_ge(sg.h, dup_d[rt - 1])
                win = UTM[:, SB_WIN[wpar]:SB_WIN[wpar] + 2048]
                for A in range(2):
                    if rt >= 2:
                        gpsimd.wait_ge(smm.h, z_mm[(rt - 2, A, 1)])
                    zw = UTM[:, SB_ZW[wpar][A]:SB_ZW[wpar][A] + 1024]
                    srcv = win[64 * A:64 * A + 64, :].rearrange("p (j r) -> p j r", j=16)
                    sg.inc(gpsimd.dma_start(
                        out=zw[0:64, :].rearrange("p (j r) -> p j r", j=8),
                        in_=srcv[:, 0:16:2, :]), 16)
                    sg.inc(gpsimd.dma_start(
                        out=zw[64:128, :].rearrange("p (j r) -> p j r", j=8),
                        in_=srcv[:, 1:16:2, :]), 16)
                dup_d[rt] = sg.n
            blk.gpsimd(dup)

        def emit_z(rt):
            wpar = rt % 2
            for A in range(2):
                for n2 in range(2):
                    g = rt * 4 + A * 2 + n2
                    par = g % 2
                    prev_key = (((g - 2) // 4, ((g - 2) % 4) // 2, (g - 2) % 2)
                                if g >= 2 else None)

                    def pe_z(tensor, rt=rt, A=A, n2=n2, par=par,
                             prev_key=prev_key, wpar=wpar):
                        tensor.wait_ge(sg.h, dup_d[rt])
                        if rt == 0 and A == 0 and n2 == 0:
                            tensor.wait_ge(sg.h, wo_ld)
                        if prev_key is not None:
                            tensor.wait_ge(sdv.h, z_ev[prev_key])
                        zw = UTM[:, SB_ZW[wpar][A]:SB_ZW[wpar][A] + 1024]
                        for jj in range(KC):
                            tensor.matmul(
                                pz[par][:], zw[:, 128 * jj:128 * (jj + 1)],
                                WS2[:, 1024 * jj + 512 * n2:1024 * jj + 512 * (n2 + 1)],
                                start=(jj == 0), stop=False)
                        ins = tensor.matmul(pz[par][:], ONEB[:],
                                            BROWS[0:1, 512 * n2:512 * (n2 + 1)],
                                            start=False, stop=True)
                        z_mm[(rt, A, n2)] = smm.inc(ins)
                    blk.tensor(pe_z)

                    def dv_z(vector, rt=rt, A=A, n2=n2, par=par):
                        vector.wait_ge(smm.h, z_mm[(rt, A, n2)])
                        ch = A * 8 + rt
                        xs = self.XM[:, 1024 * ch + 512 * n2:1024 * ch + 512 * (n2 + 1)]
                        z_ev[(rt, A, n2)] = sdv.inc(
                            vector.tensor_add(xs, xs, pz[par][:]))
                    blk.vector(dv_z)
        self.l0z_ev = z_ev

        for rt in range(8):
            for nj in range(4):
                n = 4 * rt + nj
                emit_strm(n)
                emit_vm(n)
            emit_dup(rt)
            if rt >= 1:
                emit_z(rt - 1)
        emit_z(7)

        es_rt.close()
        es_en.close()
        self.x_l0a_done = sdv.n
        self.dbg("x_l0a", self.XM[:], self.sdv)

    def phase_l0_ffn(self):
        nc, blk = self.nc, self.blk
        sq, sg, smm, sdv, sac, scc = (self.sq, self.sg, self.smm, self.sdv,
                                      self.sac, self.scc)
        FMA, UTM, SCRB, SCRF, SM = (self.FMA, self.UTM, self.SCRB,
                                    self.SCRF, self.SM)
        WS1, BROWS = self.WS1, self.BROWS

        stats = SM[:, 0:128]
        cols = self.ln_stats(
            stats, extra_minmax=True,
            chunk_wait=lambda v, t: v.wait_ge(
                sdv.h, self.l0z_ev[(t % 8, t // 8, 1)]))
        CM, CV, CI, CN, CX, CNM = (cols["m"], cols["v"], cols["inv"],
                                   cols["negmi"], cols["xmax"], cols["negxmin"])

        QA, QB, QSX, QRS = 128, 144, 160, 176

        def qstats(vector):
            m_ = stats[:, CM:CM + 16]
            t1 = SM[:, QA:QA + 16]
            t2 = SM[:, QB:QB + 16]
            sdv.inc(vector.tensor_sub(t1, stats[:, CX:CX + 16], m_))
            sdv.wait(vector)
            sdv.inc(vector.tensor_add(t2, stats[:, CNM:CNM + 16], m_))
            sdv.wait(vector)
            amax = SM[:, QSX:QSX + 16]
            sdv.inc(vector.tensor_max(amax, t1, t2))
            sdv.wait(vector)
            sdv.inc(vector.tensor_mul(amax, amax, stats[:, CI:CI + 16]))
            sdv.wait(vector)
            sdv.inc(vector.tensor_scalar_max(amax, amax, 1e-8))
            sdv.wait(vector)
            r127 = SM[:, QRS:QRS + 16]
            sdv.inc(vector.reciprocal(r127, amax))
            sdv.wait(vector)
            sdv.inc(vector.tensor_scalar_mul(r127, r127, 127.0))
            sdv.wait(vector)
            sdv.inc(vector.tensor_mul(SM[:, QA:QA + 16], stats[:, CI:CI + 16], r127))
            sdv.wait(vector)
            sdv.inc(vector.scalar_tensor_tensor(
                SM[:, QB:QB + 16], m_, -1.0, SM[:, QA:QA + 16], ALU.mult, ALU.mult))
            sdv.wait(vector)
            sdv.inc(vector.tensor_scalar_mul(SM[:, QSX:QSX + 16], amax, 1.0 / 127.0))
        blk.vector(qstats)
        qstats_done = sdv.n

        rsx_rows = BROWS[0:1, 3328:3328 + 2048]
        rsx_bf = SCRB[:, 4114:4130]

        def rsx_cast(vector):
            vector.wait_ge(sdv.h, qstats_done)
            sdv.inc(vector.tensor_copy(rsx_bf, SM[:, QRS:QRS + 16]))
        blk.vector(rsx_cast)
        rsx_bf_d = sdv.n

        es_rs = ExitStack()
        prs = es_rs.enter_context(nc.psum_tensor("prs", [1, 128], BF16))
        rs_mm = [0] * TC
        rs_ev = [0] * TC
        for t in range(TC):
            def rs_tp(tensor, t=t):
                if t == 0:
                    tensor.wait_ge(sdv.h, rsx_bf_d)
                else:
                    tensor.wait_ge(sdv.h, rs_ev[t - 1])
                rs_mm[t] = smm.inc(tensor.transpose(
                    prs[:], rsx_bf[:, t:t + 1], self.IDB[:]))
            blk.tensor(rs_tp)

            def rs_ev_f(vector, t=t):
                vector.wait_ge(smm.h, rs_mm[t])
                rs_ev[t] = sdv.inc(vector.tensor_copy(
                    rsx_rows[0:1, 128 * t:128 * (t + 1)], prs[:]))
            blk.vector(rs_ev_f)
        es_rs.close()
        rsx_done = sdv.n

        tq2 = [SCRF[:, 0:1024], SCRF[:, 1024:2048]]
        sacb = sac.n
        xq_s1 = [sacb + t + 1 for t in range(TC)]
        xq_d = [0] * TC
        for t in range(TC):
            def xq_v(vector, t=t):
                vector.wait_ge(sac.h, xq_s1[t])
                xq_d[t] = sdv.inc(vector.tensor_scalar(
                    UTM[:, 1024 * t:1024 * (t + 1)], tq2[t % 2], MAGIC, MAGIC,
                    ALU.add, ALU.subtract))
            blk.vector(xq_v)
        xq_done = sdv.n

        def xq_a_all(scalar):
            for t in range(TC):
                if t == 0:
                    scalar.wait_ge(sdv.h, qstats_done)
                if t >= 2:
                    scalar.wait_ge(sdv.h, xq_d[t - 2])
                v = sac.inc(scalar.activation(
                    tq2[t % 2], self.XM[:, 1024 * t:1024 * (t + 1)], AF.Identity,
                    bias=SM[:, QB + t:QB + t + 1], scale=SM[:, QA + t:QA + t + 1]))
                assert v == xq_s1[t]
        blk.scalar(xq_a_all)

        es_f = ExitStack()
        pfn = [es_f.enter_context(nc.psum_tensor(f"pfn{i}", [128, 1024], F32))
               for i in range(2)]
        xq_tpb = self.transpose_tm_fm(
            "xq", UTM, FMA, wait_sem=sdv, order="t",
            group_wait=lambda t0: xq_d[t0 + 3])
        xqfm_done = sdv.n

        f_mm = [0] * TC
        f_ev = [0] * TC
        for t in range(TC):
            par = t % 2

            def f_pe(tensor, t=t, par=par):
                if t == 0:
                    tensor.wait_ge(sdv.h, rsx_done)
                    tensor.wait_ge(sg.h, self.wf_back_d)
                if t % 4 == 0:
                    tensor.wait_ge(sdv.h, xq_tpb + 8 * (t // 4 + 1))
                if t >= 2:
                    tensor.wait_ge(sdv.h, f_ev[t - 2])
                for n in range(2):
                    for k in range(KC):
                        tensor.matmul(
                            pfn[par][:, 512 * n:512 * (n + 1)],
                            FMA[:, k * T + 128 * t:k * T + 128 * (t + 1)],
                            WS1[:, 1024 * k + 512 * n:1024 * k + 512 * (n + 1)],
                            start=(k == 0), stop=False)
                    ins = tensor.matmul(
                        pfn[par][:, 512 * n:512 * (n + 1)],
                        rsx_rows[0:1, 128 * t:128 * (t + 1)],
                        BROWS[0:1, D + 512 * n:D + 512 * (n + 1)],
                        start=False, stop=True)
                f_mm[t] = smm.inc(ins)
            blk.tensor(f_pe)

            def f_ev_f(vector, t=t, par=par):
                vector.wait_ge(smm.h, f_mm[t])
                xs = self.XM[:, 1024 * t:1024 * (t + 1)]
                f_ev[t] = sdv.inc(vector.scalar_tensor_tensor(
                    xs, pfn[par][:], SM[:, QSX + t:QSX + t + 1], xs,
                    ALU.mult, ALU.add))
            blk.vector(f_ev_f)
        es_f.close()
        self.x_l0f_done = sdv.n
        self.dbg("x_l0f", self.XM[:], self.sdv)

    def phase_l1_attn(self):
        nc, blk = self.nc, self.blk
        sq, sg, smm, sdv, sac, scc = (self.sq, self.sg, self.smm, self.sdv,
                                      self.sac, self.scc)
        FMA, UTM, SCRB, SCRF, SM = (self.FMA, self.UTM, self.SCRB, self.SCRF,
                                    self.SM)
        WS2, BROWS, PP = self.WS2, self.BROWS, self.PP

        def l1w(gpsimd):
            gpsimd.wait_ge(sdv.h, self.x_l0f_done)
            self.lw_inline(gpsimd, self.wi["wq1T"], WS2.ap()[:, 0:1024], D, 128)
            self.lw_inline(gpsimd, self.wi["wk1T"], WS2.ap()[:, 1024:2048], D, 128)
            self.lw_inline(gpsimd, self.wi["wv1T"], WS2.ap()[:, 2048:3072], D, 128)
            self.lw_inline(gpsimd, self.wi["wo1T"], WS2.ap()[:, 3072:4096], 128, 1024)
        blk.gpsimd(l1w)
        l1w_d = sg.n

        def ppv(vector):
            vector.wait_ge(self.sq.h, self.loads_done)
            iw = PP[:, 20:21]
            sdv.inc(vector.tensor_scalar_add(iw, PP[:, 16:17], 1e-8))
            sdv.wait(vector)
            sdv.inc(vector.reciprocal(iw, iw))
            sdv.wait(vector)
            sdv.inc(vector.scalar_tensor_tensor(
                PP[:, 21:22], PP[:, 18:19], iw, PP[:, 17:18], ALU.mult, ALU.add))
            sdv.inc(vector.scalar_tensor_tensor(
                PP[:, 22:23], PP[:, 19:20], iw, PP[:, 17:18], ALU.mult, ALU.add))
        blk.vector(ppv)
        ppv_d = sdv.n

        stats = SM[:, 0:128]
        cols = self.ln_stats(stats,
                             wait=lambda v: v.wait_ge(sdv.h, self.x_l0f_done))
        ln2_apb = self.ln_apply(stats, cols, UTM)
        ln2_tp_marks = [0] * 4

        def emit_ln2_tp(g):
            b = self.transpose_tm_fm(
                f"ln2_{g}", UTM[:, 4096 * g:4096 * (g + 1)], FMA,
                n_tc=4, wait_sem=sac, order="t",
                group_wait=lambda t0, g=g: ln2_apb + 4 * g + 4,
                t_off=4 * g, fm_kstride=2048)
            ln2_tp_marks[g] = b + 8

        es_qkv = ExitStack()
        pth = [es_qkv.enter_context(nc.psum_tensor(f"pth{i}", [128, 512], F32))
               for i in range(2)]
        TH, TT2 = 0, 2048

        def theta_of(side, dst_cos, dst_sin):
            wofs = 1024 * side
            bias_col = PP[:, 21 + side:22 + side]
            mm_marks = [0] * 4
            sacb = sac.n
            ev_marks = [sacb + n + 1 for n in range(4)]
            sin_marks = [sacb + 4 + i + 1 for i in range(4)]
            entry_ac = sacb
            entry_dv = sdv.n
            for n in range(4):
                if side == 0:
                    emit_ln2_tp(n)

                def pj(tensor, n=n, wofs=wofs):
                    if n == 0:
                        tensor.wait_ge(sg.h, l1w_d)
                        tensor.wait_ge(sac.h, entry_ac)
                        tensor.wait_ge(sdv.h, entry_dv)
                    if n >= 2:
                        tensor.wait_ge(sac.h, ev_marks[n - 2])
                    tensor.wait_ge(sdv.h, ln2_tp_marks[n])
                    for k in range(KC):
                        ins = tensor.matmul(
                            pth[n % 2][:],
                            WS2[:, wofs + 128 * k:wofs + 128 * (k + 1)],
                            FMA[:, k * T + 512 * n:k * T + 512 * (n + 1)],
                            start=(k == 0), stop=(k == KC - 1))
                    mm_marks[n] = smm.inc(ins)
                blk.tensor(pj)

            tgv_d = [0] * 4
            for i, (fn, h) in enumerate([("cos", 0), ("cos", 1),
                                         ("sin", 0), ("sin", 1)]):
                def tg_v(vector, fn=fn, h=h, i=i):
                    if i == 0:
                        vector.wait_ge(sac.h, entry_ac)
                    else:
                        vector.wait_ge(sac.h, sin_marks[i - 1])
                    vector.wait_ge(sac.h, ev_marks[2 * h + 1])
                    th = SCRF[:, TH + 1024 * h:TH + 1024 * (h + 1)]
                    tt = SCRF[:, TT2:TT2 + 1024]
                    ins = vector.tensor_scalar(
                        tt, th, INV2PI, 0.25 if fn == "cos" else 0.0,
                        ALU.mult, ALU.add)
                    sdv.inc(ins)
                    sdv.wait(vector)
                    nn = self.SCRF2[:, 0:1024]
                    sdv.inc(vector.tensor_scalar(nn, tt, MAGIC, MAGIC,
                                                 ALU.add, ALU.subtract))
                    sdv.wait(vector)
                    tgv_d[i] = sdv.inc(vector.scalar_tensor_tensor(
                        tt, nn, -TWO_PI, th, ALU.mult, ALU.add))
                blk.vector(tg_v)

            def act_all(scalar, side=side, bias_col=bias_col,
                        dst_cos=dst_cos, dst_sin=dst_sin):
                for n in range(4):
                    scalar.wait_ge(smm.h, mm_marks[n])
                    v = sac.inc(scalar.activation(
                        SCRF[:, TH + 512 * n:TH + 512 * (n + 1)], pth[n % 2][:],
                        AF.Identity, bias=bias_col, scale=PP[:, 20:21]))
                    assert v == ev_marks[n]
                for i, (fn, h) in enumerate([("cos", 0), ("cos", 1),
                                             ("sin", 0), ("sin", 1)]):
                    dst = dst_cos if fn == "cos" else dst_sin
                    scalar.wait_ge(sdv.h, tgv_d[i])
                    bias = np.float32(PI / 2) if fn == "cos" else np.float32(0.0)
                    v = sac.inc(scalar.activation(
                        dst[:, 1024 * h:1024 * (h + 1)],
                        SCRF[:, TT2:TT2 + 1024], AF.Sin, bias=bias,
                        scale=np.float32(1.0)))
                    assert v == sin_marks[i]
            blk.scalar(act_all)
            return sac.n

        self.l1_scrf_free = self.sdv.n
        cosQ = SCRB[:, 0:2048]; sinQ = SCRB[:, 2048:4096]
        q_trig_d = theta_of(0, cosQ, sinQ)
        cosK_l = SCRB[:, 4096:6144]; sinK_l = SCRB[:, 6144:8192]
        k_trig_d = theta_of(1, cosK_l, sinK_l)

        pv = [es_qkv.enter_context(nc.psum_tensor(f"pv{i}", [128, 128], F32))
              for i in range(2)]
        V_tm = UTM[:, 0:2048]
        v_mm = [0] * TC
        v_ev = [0] * TC
        for t in range(TC):
            def vp(tensor, t=t):
                if t == 0:
                    tensor.wait_ge(sg.h, l1w_d)
                if t % 4 == 0:
                    tensor.wait_ge(sdv.h, ln2_tp_marks[t // 4])
                if t >= 2:
                    tensor.wait_ge(sdv.h, v_ev[t - 2])
                for k in range(KC):
                    tensor.matmul(pv[t % 2][:],
                                  FMA[:, k * T + 128 * t:k * T + 128 * (t + 1)],
                                  WS2[:, 2048 + 128 * k:2048 + 128 * (k + 1)],
                                  start=(k == 0), stop=False)
                ins = tensor.matmul(pv[t % 2][:], self.ONEB[:],
                                    BROWS[0:1, 3072:3072 + 128],
                                    start=False, stop=True)
                v_mm[t] = smm.inc(ins)
            blk.tensor(vp)

            def vev(vector, t=t):
                vector.wait_ge(smm.h, v_mm[t])
                v_ev[t] = sdv.inc(vector.tensor_copy(
                    V_tm[:, 128 * t:128 * (t + 1)], pv[t % 2][:]))
            blk.vector(vev)
        v_d = sdv.n

        CKO, SKO, VO = 0, 128 * T, 2 * 128 * T
        kvflat = self.d_kv_in.rearrange("a b -> (a b)")

        def kv_bw(sync):
            sync.wait_ge(sac.h, k_trig_d)
            sync.wait_ge(sq.h, sq.n)
            sq.inc(sync.dma_start(
                out=kvflat[CKO:CKO + 128 * T].rearrange("(p c) -> p c", p=128),
                in_=cosK_l), 16)
            sq.inc(sync.dma_start(
                out=kvflat[SKO:SKO + 128 * T].rearrange("(p c) -> p c", p=128),
                in_=sinK_l), 16)
            sync.wait_ge(sdv.h, v_d)
            dstv = kvflat[VO:VO + T * 128].rearrange(
                "(tt p h) -> p tt h", p=128, h=128)
            sq.inc(sync.dma_start(
                out=dstv, in_=V_tm.rearrange("p (tt h) -> p tt h", h=128)), 16)
        blk.sync(kv_bw)
        kv_bw_d = sq.n

        def kv_cc(gpsimd):
            gpsimd.wait_ge(sq.h, kv_bw_d)
            self.scc4.inc(gpsimd.collective_compute(
                "AllGather", ALU.bypass, replica_groups=[list(range(NCORES))],
                ins=[self.d_kv_in[:]], outs=[self.d_kv_out[:]]))
        blk.gpsimd(kv_cc)
        kv_cc_d = self.scc4.n
        es_qkv.close()

        SB_CK = [2048, 4096]
        SB_SK = [6144, 8192]
        SB_V = [10240, 12352]
        SB_PT = [4096, 5120, 6144, 7168]

        kvr = self.d_kv_out.rearrange("(r a) b -> r (a b)", r=NCORES)

        es_sim = ExitStack()
        pav = [es_sim.enter_context(nc.psum_tensor(f"pav{i}", [128, 512], F32))
               for i in range(3)]
        pavt = es_sim.enter_context(nc.psum_tensor("pavt", [128, 128], BF16))

        def av_slot(qt):
            b, s = divmod(qt, 3)
            return pav[b][:, 129 * s:129 * s + 129]

        kv_strm = {}
        sim_mm = {}
        exp_d = {}
        av_mm = {}
        onesset = [0, 0]

        def ones_pre(vector):
            for par in range(2):
                vb = UTM[:, SB_V[par]:SB_V[par] + 2112]
                sdv.inc(vector.memset(
                    vb.rearrange("p (kl c) -> p kl c", c=132)[:, :, 128:129], 1.0))
        blk.vector(ones_pre)
        ones_d = sdv.n

        self.l1_z_done = [None, None]
        self._pz = {}
        for pa in range(2):
            es_psim = ExitStack()
            psim = [es_psim.enter_context(
                nc.psum_tensor(f"psim{pa}_{i}", [128, 1024], F32))
                for i in range(2)]

            def emit_kvst(R, pa=pa):
                par = R % 2
                sc = pa * NCORES + R

                def kvst(sync, pa=pa, R=R, par=par, sc=sc):
                    if sc == 0:
                        sync.wait_ge(self.scc4.h, kv_cc_d)
                        sync.wait_ge(sdv.h, ones_d)
                    if sc == 1:
                        sync.wait_ge(smm.h, sim_mm[(0, 0)])
                        sync.wait_ge(sdv.h, ones_d)
                    prev = sc - 2
                    if prev >= 0:
                        pR = prev % NCORES
                        sync.wait_ge(smm.h, av_mm[(prev // NCORES, pR, 15)])
                    blk_base = kvr[R]
                    sp = self.sqP[par]
                    sp.inc(sync.dma_start(
                        out=UTM[:, SB_CK[par]:SB_CK[par] + 2048],
                        in_=blk_base[CKO:CKO + 128 * T].rearrange(
                            "(p c) -> p c", p=128)), 16)
                    sp.inc(sync.dma_start(
                        out=UTM[:, SB_SK[par]:SB_SK[par] + 2048],
                        in_=blk_base[SKO:SKO + 128 * T].rearrange(
                            "(p c) -> p c", p=128)), 16)
                    vb = UTM[:, SB_V[par]:SB_V[par] + 2112].rearrange(
                        "p (kl c) -> p kl c", c=132)[:, :, 0:128]
                    sp.inc(sync.dma_start(
                        out=vb,
                        in_=blk_base[VO:VO + T * 128].rearrange(
                            "(kl p h) -> p kl h", p=128, h=128)), 16)
                    kv_strm[sc] = sp.n
                blk.sync(kvst)

            def emit_sim(it, pa=pa, psim=psim):
                R, kl = divmod(it, 16)
                par, kpar, sc = R % 2, it % 2, pa * NCORES + R

                def sim_f(tensor, R=R, kl=kl, par=par, kpar=kpar, sc=sc, it=it):
                    if kl == 0:
                        tensor.wait_ge(self.sqP[par].h, kv_strm[sc])
                    if pa == 1 and it == 0:
                        tensor.wait_ge(sac.h, exp_d[(0, 127)])
                    if it >= 2:
                        tensor.wait_ge(sac.h, exp_d[(pa, it - 2)])
                    ck = UTM[:, SB_CK[par] + 128 * kl:SB_CK[par] + 128 * (kl + 1)]
                    sk = UTM[:, SB_SK[par] + 128 * kl:SB_SK[par] + 128 * (kl + 1)]
                    for n in range(2):
                        tensor.matmul(psim[kpar][:, 512 * n:512 * (n + 1)],
                                      ck, cosQ[:, 1024 * pa + 512 * n:
                                               1024 * pa + 512 * (n + 1)],
                                      start=True, stop=False)
                        ins = tensor.matmul(
                            psim[kpar][:, 512 * n:512 * (n + 1)],
                            sk, sinQ[:, 1024 * pa + 512 * n:
                                     1024 * pa + 512 * (n + 1)],
                            start=False, stop=True)
                    sim_mm[(sc, kl)] = smm.inc(ins)
                blk.tensor(sim_f)

            sac_base = sac.n
            for it_ in range(128):
                exp_d[(pa, it_)] = sac_base + it_ + 1

            def exp_all(scalar, pa=pa, psim=psim):
                for it in range(128):
                    R, kl = divmod(it, 16)
                    kpar, sc = it % 2, pa * NCORES + R
                    ptj = it % 4
                    scalar.wait_ge(smm.h, sim_mm[(sc, kl)])
                    if it >= 4:
                        g2 = it - 4
                        scalar.wait_ge(smm.h, av_mm[(pa, g2 // 16, g2 % 16)])
                    v = sac.inc(scalar.activation(
                        SCRB[:, SB_PT[ptj]:SB_PT[ptj] + 1024], psim[kpar][:],
                        AF.Exp, bias=np.float32(-CEXP), scale=np.float32(RS128)))
                    assert v == exp_d[(pa, it)]

            def emit_av(it, pa=pa):
                R, kl = divmod(it, 16)
                par, kpar = R % 2, it % 2

                def av_f(tensor, R=R, kl=kl, kpar=kpar, par=par, it=it):
                    tensor.wait_ge(sac.h, exp_d[(pa, it)])
                    if pa == 1 and it == 0:
                        tensor.wait_ge(sdv.h, self._pz[0]["avev_done"])
                    pt = SCRB[:, SB_PT[it % 4]:SB_PT[it % 4] + 1024]
                    vb = UTM[:, SB_V[par] + 132 * kl:SB_V[par] + 132 * kl + 129]
                    first = (R == 0 and kl == 0)
                    last = (R == NCORES - 1 and kl == 15)
                    for qt in range(8):
                        ins = tensor.matmul(
                            av_slot(qt), pt[:, 128 * qt:128 * (qt + 1)], vb,
                            start=(first and qt % 3 == 0),
                            stop=(last and qt in (2, 5, 7)))
                    av_mm[(pa, R, kl)] = smm.inc(ins)
                blk.tensor(av_f)

            emit_kvst(0)
            emit_sim(0)
            for it in range(128):
                if it + 1 < 128:
                    if (it + 1) % 16 == 0:
                        emit_kvst((it + 1) // 16)
                    emit_sim(it + 1)
                    if pa == 1 and thunks0:
                        thunks0.pop(0)()
                emit_av(it)
            blk.scalar(exp_all)

            es_psim.close()
            thunks0 = self._l1_pass_evac(pa, av_slot, av_mm, pavt)
            if pa == 1:
                for t_ in thunks0:
                    t_()
                thunks0 = []

        self._l1_tail_z()
        es_sim.close()
        self.x_l1a_done = sdv.n
        self.dbg("x_l1a", self.XM[:], self.sdv)

    def _l1_pass_evac(self, pa, av_slot, av_mm, pavt):
        nc, blk = self.nc, self.blk
        smm, sdv = self.smm, self.sdv
        SCRB, SM, BROWS = self.SCRB, self.SM, self.BROWS
        if pa == 0:
            av_sb = lambda qt: self.UTM[:, 1024 + 128 * qt:1024 + 128 * (qt + 1)]
            avT_sb = lambda qt: self.UTM[:, 14464 + 128 * qt:14464 + 128 * (qt + 1)]
        else:
            av_sb = lambda qt: SCRB[:, 6144 + 128 * qt:6144 + 128 * (qt + 1)]
            avT_sb = lambda qt: SCRB[:, 7168 + 128 * qt:7168 + 128 * (qt + 1)]
        lcol = SM[:, 128 + 16 * pa:128 + 16 * pa + 8]
        rcol = SM[:, 160 + 16 * pa:160 + 16 * pa + 8]
        lbf = self.UTM[:, 8 * pa:8 * (pa + 1)]
        lrow = BROWS[0:1, 3328 + 1024 * pa:3328 + 1024 * pa + 1024]
        avdone = av_mm[(pa, NCORES - 1, 15)]

        av_ev = [0] * 8
        for qt in range(8):
            def avev(vector, qt=qt):
                if qt == 0:
                    vector.wait_ge(smm.h, avdone)
                    if pa == 1:
                        vector.wait_ge(smm.h, self._pz[0]["avt_mm"][7])
                sl = av_slot(qt)
                sdv.inc(vector.tensor_copy(av_sb(qt), sl[:, 0:128]))
                sdv.wait(vector)
                sdv.inc(vector.tensor_copy(lcol[:, qt:qt + 1], sl[:, 128:129]))
                sdv.wait(vector)
                av_ev[qt] = sdv.inc(vector.reciprocal(rcol[:, qt:qt + 1],
                                                      lcol[:, qt:qt + 1]))
            blk.vector(avev)
        avev_done = sdv.n

        def lbf_cast(vector):
            sdv.inc(vector.tensor_copy(lbf, lcol))
        blk.vector(lbf_cast)
        lbf_d = sdv.n

        st = dict(av_ev=av_ev, avev_done=avev_done, avt_mm=[0] * 8,
                  avt_ev=[0] * 8, lr_mm=[0] * 8, lr_ev=[0] * 8,
                  rcol=rcol, lrow=lrow, avT_sb=avT_sb)
        self._pz[pa] = st
        plr = pavt[0:1, 0:128]

        thunks = []
        for qt in range(8):
            def em_avt(qt=qt):
                def avt(tensor, qt=qt):
                    tensor.wait_ge(sdv.h, st["av_ev"][qt])
                    if qt >= 1:
                        tensor.wait_ge(sdv.h, st["lr_ev"][qt - 1])
                    elif pa == 1:
                        tensor.wait_ge(sdv.h, self._pz[0]["lr_ev"][7])
                    st["avt_mm"][qt] = smm.inc(tensor.transpose(
                        pavt[:], av_sb(qt), self.IDB[:]))
                blk.tensor(avt)
            thunks.append(em_avt)

            def em_avtev(qt=qt):
                def avtev(vector, qt=qt):
                    vector.wait_ge(smm.h, st["avt_mm"][qt])
                    st["avt_ev"][qt] = sdv.inc(
                        vector.tensor_copy(avT_sb(qt), pavt[:]))
                blk.vector(avtev)
            thunks.append(em_avtev)

            def em_lrt(qt=qt):
                def lrt(tensor, qt=qt):
                    tensor.wait_ge(sdv.h, lbf_d)
                    tensor.wait_ge(sdv.h, st["avt_ev"][qt])
                    st["lr_mm"][qt] = smm.inc(tensor.transpose(
                        plr, lbf[:, qt:qt + 1], self.IDB[:]))
                blk.tensor(lrt)
            thunks.append(em_lrt)

            def em_lrtev(qt=qt):
                def lrtev(vector, qt=qt):
                    vector.wait_ge(smm.h, st["lr_mm"][qt])
                    st["lr_ev"][qt] = sdv.inc(vector.tensor_copy(
                        lrow[0:1, 128 * qt:128 * (qt + 1)], plr))
                blk.vector(lrtev)
            thunks.append(em_lrtev)
        return thunks

    def _l1_tail_z(self):
        nc, blk = self.nc, self.blk
        smm, sdv = self.smm, self.sdv
        WS2, BROWS = self.WS2, self.BROWS
        es_z = ExitStack()
        pzz = [es_z.enter_context(nc.psum_tensor(f"pzz{i}", [128, 1024], F32))
               for i in range(2)]
        z_mm = {}
        z_ev = {}
        for g in range(16):
            pa, qt = divmod(g, 8)
            zpar = g % 2
            st = self._pz[pa]

            def zf(tensor, pa=pa, qt=qt, zpar=zpar, g=g, st=st):
                tensor.wait_ge(sdv.h, st["avt_ev"][qt])
                tensor.wait_ge(sdv.h, st["lr_ev"][qt])
                if g >= 2:
                    tensor.wait_ge(sdv.h, z_ev[g - 2])
                for n in range(2):
                    tensor.matmul(pzz[zpar][:, 512 * n:512 * (n + 1)],
                                  st["avT_sb"](qt),
                                  WS2[:, 3072 + 512 * n:3072 + 512 * (n + 1)],
                                  start=True, stop=False)
                    ins = tensor.matmul(
                        pzz[zpar][:, 512 * n:512 * (n + 1)],
                        st["lrow"][0:1, 128 * qt:128 * (qt + 1)],
                        BROWS[0:1, 2048 + 512 * n:2048 + 512 * (n + 1)],
                        start=False, stop=True)
                z_mm[g] = smm.inc(ins)
            blk.tensor(zf)

            def zev(vector, pa=pa, qt=qt, zpar=zpar, g=g, st=st):
                vector.wait_ge(smm.h, z_mm[g])
                t_g = 8 * pa + qt
                xs = self.XM[:, 1024 * t_g:1024 * (t_g + 1)]
                z_ev[g] = sdv.inc(vector.scalar_tensor_tensor(
                    xs, pzz[zpar][:], st["rcol"][:, qt:qt + 1], xs,
                    ALU.mult, ALU.add))
            blk.vector(zev)
        es_z.close()
        self.l1z_ev = z_ev
        self.l1_z_done = [None, sdv.n]

    def phase_l1_ffn(self):
        nc, blk = self.nc, self.blk
        sq, sg, smm, sdv, sac = self.sq, self.sg, self.smm, self.sdv, self.sac
        FMA, UTM, SCRB, SCRF, SM, PP = (self.FMA, self.UTM, self.SCRB,
                                        self.SCRF, self.SM, self.PP)
        WS1, WS2 = self.WS1, self.WS2

        stats = SM[:, 0:128]
        cols = self.ln_stats(
            stats, chunk_wait=lambda v, t: v.wait_ge(sdv.h, self.l1z_ev[t]))
        ln3_stats_d = sdv.n

        def halo_ap(scalar):
            sdv.wait(scalar)
            self.sac.inc(scalar.activation(
                UTM[:, 0:1024], self.XM[:, 1024 * 15:1024 * 16], AF.Identity,
                bias=stats[:, cols["negmi"] + 15:cols["negmi"] + 16],
                scale=stats[:, cols["inv"] + 15:cols["inv"] + 16]))
        blk.scalar(halo_ap)
        halo_ap_d = sac.n

        def halo_bw(sync):
            sync.wait_ge(sac.h, halo_ap_d)
            sync.wait_ge(sq.h, sq.n)
            sq.inc(sync.dma_start(out=self.d_halo_in[:], in_=UTM[127:128, 0:1024]), 16)
        blk.sync(halo_bw)
        halo_bw_d = sq.n

        def halo_cc(gpsimd):
            gpsimd.wait_ge(sq.h, halo_bw_d)
            self.scc.inc(gpsimd.collective_compute(
                "AllGather", ALU.bypass, replica_groups=[list(range(NCORES))],
                ins=[self.d_halo_in[:]], outs=[self.d_halo_out[:]]))
        blk.gpsimd(halo_cc)
        halo_cc_d = self.scc.n

        hg = SCRB[0:8, 0:1024]

        def halo_rd(sync):
            sync.wait_ge(self.scc.h, halo_cc_d)
            sync.wait_ge(sq.h, sq.n)
            sq.inc(sync.dma_start(out=hg, in_=self.d_halo_out[:]), 16)
        blk.sync(halo_rd)
        halo_rd_d = sq.n

        es_h = ExitStack()
        phl = es_h.enter_context(nc.psum_tensor("phl", [128, 1], F32))
        hl_mm = [0] * KC
        hl_ev = [0] * KC
        for k in range(KC):
            def hlt(tensor, k=k):
                if k == 0:
                    tensor.wait_ge(sq.h, halo_rd_d)
                else:
                    tensor.wait_ge(sdv.h, hl_ev[k - 1])
                hl_mm[k] = smm.inc(tensor.matmul(
                    phl[:], hg[:, 128 * k:128 * (k + 1)], self.HSEL[:],
                    start=True, stop=True))
            blk.tensor(hlt)

            def hlev(vector, k=k):
                vector.wait_ge(smm.h, hl_mm[k])
                hl_ev[k] = sdv.inc(vector.tensor_copy(PP[:, 24 + k:25 + k], phl[:]))
            blk.vector(hlev)
        es_h.close()
        halo_done = sdv.n

        wr_ld = [0, 0]
        wk_ld = [0, 0]
        wv_ld = [0, 0]
        h_done = [0, 0]
        for H in range(2):
            t0 = 8 * H

            if H == 0:
                def ldw(gpsimd):
                    gpsimd.wait_ge(sg.h, sg.n)
                    gpsimd.wait_ge(sdv.h, self.x_l1a_done)
                    self.lw_inline(gpsimd, self.wi["wr1T"], WS1.ap(), D, D)
                    self.lw_inline(gpsimd, self.wi["wkf1T"], WS2.ap(), D, D)
                    wr_ld[0] = wk_ld[0] = sg.n
                blk.gpsimd(ldw)
            else:
                wr_ld[1] = wr_ld[0]
                wk_ld[1] = wk_ld[0]

            self.ln_apply(stats, cols, UTM[:, 0:8192], n0=t0, n1=t0 + 8)
            ap_d = sac.n

            self.transpose_tm_fm(f"u{H}", UTM[:, 0:8192], FMA[:, 0:8192],
                                 n_tc=8, wait_sem=sac, wait_val=ap_d)
            u_d = sdv.n

            xr = UTM[:, 8192:16384]
            xk = UTM[:, 0:8192]
            dsc = SCRF[:, 0:1024]
            mix_d = [0] * KC
            for k in range(KC):
                def mix(vector, k=k, H=H):
                    if k == 0:
                        vector.wait_ge(sdv.h, u_d)
                        if H == 0:
                            vector.wait_ge(sdv.h, halo_done)
                    else:
                        vector.wait_ge(sdv.h, mix_d[k - 1])
                    u = FMA[:, 1024 * k:1024 * (k + 1)]
                    sdv.inc(vector.tensor_sub(dsc[:, 1:1024], u[:, 1:1024],
                                              u[:, 0:1023]))
                    sdv.inc(vector.tensor_sub(dsc[:, 0:1], u[:, 0:1],
                                              PP[:, 24 + k:25 + k]))
                    sdv.wait(vector)
                    sdv.inc(vector.scalar_tensor_tensor(
                        xr[:, 1024 * k + 1:1024 * (k + 1)], dsc[:, 1:1024],
                        PP[:, k:k + 1], u[:, 0:1023], ALU.mult, ALU.add))
                    sdv.inc(vector.scalar_tensor_tensor(
                        xr[:, 1024 * k:1024 * k + 1], dsc[:, 0:1],
                        PP[:, k:k + 1], PP[:, 24 + k:25 + k], ALU.mult, ALU.add))
                    sdv.inc(vector.scalar_tensor_tensor(
                        xk[:, 1024 * k + 1:1024 * (k + 1)], dsc[:, 1:1024],
                        PP[:, 8 + k:9 + k], u[:, 0:1023], ALU.mult, ALU.add))
                    ins = vector.scalar_tensor_tensor(
                        xk[:, 1024 * k:1024 * k + 1], dsc[:, 0:1],
                        PP[:, 8 + k:9 + k], PP[:, 24 + k:25 + k], ALU.mult, ALU.add)
                    mix_d[k] = sdv.inc(ins)
                blk.vector(mix)
            mixes_d = sdv.n

            if H == 0:
                def h2halo(vector):
                    sdv.wait(vector)
                    for k in range(KC):
                        sdv.inc(vector.tensor_copy(
                            PP[:, 24 + k:25 + k],
                            FMA[:, 1024 * k + 1023:1024 * (k + 1)]))
                blk.vector(h2halo)
            h2halo_d = sdv.n

            def ldv(gpsimd, H=H, h2halo_d=h2halo_d):
                gpsimd.wait_ge(sg.h, sg.n)
                gpsimd.wait_ge(sdv.h, h2halo_d)
                self.lw_inline(gpsimd, self.wi["wvf1T"],
                               FMA[:, 0:8192], D, D)
                wv_ld[H] = sg.n
            blk.gpsimd(ldv)

            es_r = ExitStack()
            pr = [es_r.enter_context(nc.psum_tensor(f"pr{H}{i}", [128, 1024], F32))
                  for i in range(2)]
            r_mm = [0] * 8
            sacb_r = sac.n
            r_ev = [sacb_r + t + 1 for t in range(8)]
            for t in range(8):
                def rp(tensor, t=t, H=H):
                    if t == 0:
                        tensor.wait_ge(sdv.h, mixes_d)
                        tensor.wait_ge(sg.h, wr_ld[H])
                    if t >= 2:
                        tensor.wait_ge(sac.h, r_ev[t - 2])
                    for n in range(2):
                        for k in range(KC):
                            ins = tensor.matmul(
                                pr[t % 2][:, 512 * n:512 * (n + 1)],
                                xr[:, 1024 * k + 128 * t:1024 * k + 128 * (t + 1)],
                                WS1[:, 1024 * k + 512 * n:1024 * k + 512 * (n + 1)],
                                start=(k == 0), stop=(k == KC - 1))
                    r_mm[t] = smm.inc(ins)
                blk.tensor(rp)

            def rev_all(scalar, H=H):
                for t in range(8):
                    scalar.wait_ge(smm.h, r_mm[t])
                    v = sac.inc(scalar.activation(
                        SCRB[:, 1024 * t:1024 * (t + 1)], pr[t % 2][:],
                        AF.Sigmoid))
                    assert v == r_ev[t]
            blk.scalar(rev_all)
            r_d = sac.n
            es_r.close()

            es_k = ExitStack()
            pk2 = [es_k.enter_context(nc.psum_tensor(f"pk2{H}{i}", [128, 1024], F32))
                   for i in range(2)]
            krel2 = [SCRF[:, 1024:2048], self.SCRF2[:, 0:1024]]
            k_mm = [0] * KC
            krel_d = [0] * KC
            sacb_k = sac.n
            k_ev = [sacb_k + dk + 1 for dk in range(KC)]
            for dk in range(KC):
                def kp(tensor, dk=dk, H=H):
                    if dk == 0:
                        tensor.wait_ge(sdv.h, mixes_d)
                        tensor.wait_ge(sg.h, wk_ld[H])
                        tensor.wait_ge(sac.h, r_ev[7])
                    if dk >= 2:
                        tensor.wait_ge(sac.h, k_ev[dk - 2])
                    for n in range(2):
                        for k in range(KC):
                            ins = tensor.matmul(
                                pk2[dk % 2][:, 512 * n:512 * (n + 1)],
                                WS2[:, 1024 * k + 128 * dk:1024 * k + 128 * (dk + 1)],
                                xk[:, 1024 * k + 512 * n:1024 * k + 512 * (n + 1)],
                                start=(k == 0), stop=(k == KC - 1))
                    k_mm[dk] = smm.inc(ins)
                blk.tensor(kp)

                def kev(vector, dk=dk):
                    vector.wait_ge(smm.h, k_mm[dk])
                    if dk >= 2:
                        vector.wait_ge(sac.h, k_ev[dk - 2])
                    krel_d[dk] = sdv.inc(vector.tensor_scalar_max(
                        krel2[dk % 2], pk2[dk % 2][:], 0.0))
                blk.vector(kev)

            def kev2_all(scalar, H=H):
                for dk in range(KC):
                    scalar.wait_ge(sdv.h, krel_d[dk])
                    v = sac.inc(scalar.activation(
                        FMA[:, 8192 + 1024 * dk:8192 + 1024 * (dk + 1)],
                        krel2[dk % 2], AF.Square))
                    assert v == k_ev[dk]
            blk.scalar(kev2_all)
            kact_d = sac.n
            es_k.close()

            es_v = ExitStack()
            pv2 = [es_v.enter_context(nc.psum_tensor(f"pv2{H}{i}", [128, 1024], F32))
                   for i in range(2)]
            vsc2 = [SCRF[:, 0:1024], self.SCRF2[:, 0:1024]]
            v_mm = [0] * 8
            v_ev = [0] * 8
            for t in range(8):
                def vp2(tensor, t=t, H=H):
                    if t == 0:
                        tensor.wait_ge(sac.h, kact_d)
                        tensor.wait_ge(sg.h, wv_ld[H])
                    if t >= 2:
                        tensor.wait_ge(sdv.h, v_ev[t - 2])
                    for n in range(2):
                        for k in range(KC):
                            ins = tensor.matmul(
                                pv2[t % 2][:, 512 * n:512 * (n + 1)],
                                FMA[:, 8192 + 1024 * k + 128 * t:
                                    8192 + 1024 * k + 128 * (t + 1)],
                                FMA[:, 1024 * k + 512 * n:1024 * k + 512 * (n + 1)],
                                start=(k == 0), stop=(k == KC - 1))
                    v_mm[t] = smm.inc(ins)
                blk.tensor(vp2)

                def vev(vector, t=t, H=H):
                    vector.wait_ge(smm.h, v_mm[t])
                    vector.wait_ge(sac.h, r_d)
                    vsc = vsc2[t % 2]
                    sdv.inc(vector.tensor_mul(vsc, pv2[t % 2][:],
                                              SCRB[:, 1024 * t:1024 * (t + 1)]))
                    sdv.wait(vector)
                    tg = 8 * H + t
                    xs = self.XM[:, 1024 * tg:1024 * (tg + 1)]
                    v_ev[t] = sdv.inc(vector.tensor_add(xs, xs, vsc))
                blk.vector(vev)
            es_v.close()
            h_done[H] = sdv.n

        self.x_l1f_done = sdv.n
        self.dbg("x_l1f", self.XM[:], self.sdv)


QKPERM = np.zeros(D, np.int64)
for _dg in range(8):
    for _a in range(16):
        for _di in range(8):
            QKPERM[128 * _dg + 8 * _a + _di] = 64 * _a + 8 * _dg + _di


def make_host_inputs(inp):
    x = np.asarray(inp["L0_attn_x"], np.float32).reshape(S, D)
    c = np.ascontiguousarray
    ident = np.eye(128, dtype=np.float32)
    mask_dd = np.zeros((128, 128), np.float32)
    for m in range(128):
        mask_dd[m, m % 8::8] = 1.0
    e16 = np.zeros((128, 16), np.float32)
    for a in range(16):
        e16[8 * a:8 * a + 8, a] = 1.0
    m128 = np.zeros((128, 128), np.float32)
    for b2 in range(2):
        for A in range(2):
            for dd in range(64):
                m128[64 * b2 + dd, 64 * A + dd] = 1.0
    s_allm = np.zeros((16, KC * 128), np.float32)
    for p in range(KC):
        for beta in range(2):
            s_allm[2 * p + beta, 128 * p + 64 * beta:128 * p + 64 * beta + 64] = 1.0
    ones_row = np.ones((1, 128), np.float32)

    import ml_dtypes
    bf = lambda a: np.asarray(a, np.float32).astype(ml_dtypes.bfloat16)

    maps = []
    for core in range(NCORES):
        hsm = np.zeros((16, 2), np.float32)
        hsm[2 * core, 0] = 1.0
        hsm[2 * core + 1, 1] = 1.0
        hsel = np.zeros((8, 1), np.float32)
        if core > 0:
            hsel[core - 1, 0] = 1.0
        m = {
            "x_slice": c(x[T * core:T * (core + 1)]),
            "L0_Wq": c(np.asarray(inp["L0_attn_Wq"], np.float32)[:, QKPERM]),
            "L0_Wk": c(np.asarray(inp["L0_attn_Wk"], np.float32)[:, QKPERM]),
            "L0_WvT": c(np.asarray(inp["L0_attn_Wv"], np.float32).T),
            "L0_Wo": c(np.asarray(inp["L0_attn_Wo"], np.float32)),
            "L0_bo_row": c(np.asarray(inp["L0_attn_bo"], np.float32).reshape(1, D)),
            "F0_w_slab": c(np.asarray(inp["L0_ffn_weight"], np.float32)[128 * core:128 * (core + 1)]),
            "F0_bias_row": c(np.asarray(inp["L0_ffn_bias"], np.float32).reshape(1, D)),
            "L1_WqT": c(np.asarray(inp["L1_attn_Wq"], np.float32).T),
            "L1_WkT": c(np.asarray(inp["L1_attn_Wk"], np.float32).T),
            "L1_WvT": c(np.asarray(inp["L1_attn_Wv"], np.float32).T),
            "L1_WoT": c(np.asarray(inp["L1_attn_Wo"], np.float32).T),
            "L1_bq_pp": c(np.asarray(inp["L1_attn_bq"], np.float32).reshape(128, 1)),
            "L1_bk_pp": c(np.asarray(inp["L1_attn_bk"], np.float32).reshape(128, 1)),
            "L1_bv_row": c(np.asarray(inp["L1_attn_bv"], np.float32).reshape(1, 128)),
            "L1_bo_row": c(np.asarray(inp["L1_attn_bo"], np.float32).reshape(1, D)),
            "L1_wav_pp": c(np.asarray(inp["L1_attn_wavelengths"], np.float32).reshape(128, 1)),
            "L1_phb_pp": c(np.asarray(inp["L1_attn_phase_bias"], np.float32).reshape(128, 1)),
            "F1_WrT": c(np.asarray(inp["L1_ffn_W_r"], np.float32).T),
            "F1_WkT": c(np.asarray(inp["L1_ffn_W_k"], np.float32).T),
            "F1_WvT": c(np.asarray(inp["L1_ffn_W_v"], np.float32).T),
            "F1_mur_pp": c(np.asarray(inp["L1_ffn_mu_r"], np.float32).reshape(KC, 128).T),
            "F1_muk_pp": c(np.asarray(inp["L1_ffn_mu_k"], np.float32).reshape(KC, 128).T),
            "identb": bf(ident),
            "mask_dd": bf(mask_dd),
            "e16sel": bf(e16),
            "mask128": bf(m128),
            "s_all": bf(s_allm),
            "hs": bf(hsm),
            "ones_col": ones_row,
            "ones_col_b": bf(ones_row),
            "halo_sel": bf(hsel),
        }
        maps.append(m)
    return maps


_MK_CACHE = {}


def _get_mega():
    if "mk" not in _MK_CACHE:
        _MK_CACHE["mk"] = Mega(upto=5, debug=())
    return _MK_CACHE["mk"]


def kernel(**inputs):
    mk = _get_mega()
    maps = make_host_inputs(inputs)
    res = run_bass_kernel_spmd(mk.nc, maps, core_ids=list(range(NCORES)))
    out = np.concatenate([np.asarray(res.results[c]["y"], np.float32)
                          for c in range(NCORES)], axis=0)
    return out.reshape(1, S, D)
